# version 04
# baseline (speedup 1.0000x reference)
import numpy as np
from contextlib import ExitStack

import concourse.bass as bass
import concourse.mybir as mybir
from concourse.bass_utils import run_bass_kernel_spmd

F32 = mybir.dt.float32
BF16 = mybir.dt.bfloat16
I32 = mybir.dt.int32
ALU = mybir.AluOpType
AF = mybir.ActivationFunctionType
ds = bass.ds

NC = 8
DM = 1024
DI = 2048
DS = 16
HID = 3072
L = 16384
TB = L // NC
EX = 64
TA = TB + EX
DSL = DI // NC
LNEPS = 1e-5
QSCALE = 1.0 / 127.0

ACH = [(0, 512), (512, 512), (1024, 512), (1536, 512), (2048, 64)]
SCT = 1024
NSC = L // SCT
FPW = 512
NFP = TB // FPW

DEBUG = False


class Ctr:
    def __init__(self, h):
        self.h = h
        self.n = 0

    def bump(self, amt):
        self.n += amt
        return self.n


class Sched:
    ENGS = ("sync", "gpsimd", "vector", "scalar", "tensor")

    def __init__(self):
        self.ops = {e: [] for e in self.ENGS}

    def add(self, eng, fn, waits=(), inc=None):
        waits = [(c, v) for (c, v) in waits if v > 0]
        for c, v in waits:
            assert v <= c.n, f"wait on future count {v} > {c.n}"
        entry = [fn, waits, None]
        if inc is not None:
            c, amt = inc
            c.bump(amt)
            entry[2] = (c, amt)
        self.ops[eng].append(entry)

    def emit(self, eng_obj, eng_name):
        for fn, waits, inc in self.ops[eng_name]:
            for c, v in waits:
                eng_obj.wait_ge(c.h, v)
            ins = fn(eng_obj)
            if inc is not None:
                c, amt = inc
                assert ins is not None
                ins.then_inc(c.h, amt)


def build_nc():
    nc = bass.Bass("TRN2", target_bir_lowering=False,
                   detect_race_conditions=False)

    xt = nc.dram_tensor("xt", [DM, TA], F32, kind="ExternalInput")
    wit = nc.dram_tensor("wit", [32 * 8 * 128, 128], F32, kind="ExternalInput")
    wxt = nc.dram_tensor("wxt", [DI, 80], F32, kind="ExternalInput")
    cwt = nc.dram_tensor("cwt", [DI, 4], F32, kind="ExternalInput")
    cbt = nc.dram_tensor("cbt", [DI, 1], F32, kind="ExternalInput")
    dpar = nc.dram_tensor("dpar", [DI, 1], F32, kind="ExternalInput")
    wdt = nc.dram_tensor("wdt", [DSL, 1], F32, kind="ExternalInput")
    bdt = nc.dram_tensor("bdt", [DSL, 1], F32, kind="ExternalInput")
    alog = nc.dram_tensor("alog", [DSL, DS], F32, kind="ExternalInput")
    wg_t = nc.dram_tensor("wg_t", [24 * 8 * 128, 128], I32, kind="ExternalInput")
    wu_t = nc.dram_tensor("wu_t", [24 * 8 * 128, 128], I32, kind="ExternalInput")
    wd_t = nc.dram_tensor("wd_t", [8 * 24 * 128, 128], I32, kind="ExternalInput")
    wot = nc.dram_tensor("wot", [8 * 16 * 128, 128], F32, kind="ExternalInput")
    ones_c = nc.dram_tensor("ones_c", [128, 1], F32, kind="ExternalInput")
    ones_r = nc.dram_tensor("ones_r", [1, 128], F32, kind="ExternalInput")
    ident = nc.dram_tensor("ident", [128, 128], F32, kind="ExternalInput")
    oneh = nc.dram_tensor("oneh", [16, 16 * 128], F32, kind="ExternalInput")
    cvec = nc.dram_tensor("cvec", [1, 1], I32, kind="ExternalInput")
    yoff = nc.dram_tensor("yoff", [1, 3], I32, kind="ExternalInput")
    ot = nc.dram_tensor("ot", [DM, TB], F32, kind="ExternalOutput")

    PCW = [448, 512, 512, 512, 64]
    PC0 = [0, 448, 960, 1472, 1984]
    xcv_p = [nc.dram_tensor(f"xcv_p{k}", [DI, PCW[k]], BF16)
             for k in range(5)]
    xcv_gk = [nc.dram_tensor(f"xcv_g{k}", [NC * DI, PCW[k]], BF16)
              for k in range(5)]
    xp_in = nc.dram_tensor("xp_in", [80, TB], F32)
    xp_g = nc.dram_tensor("xp_g", [NC * 80, TB], F32)
    y_inA = nc.dram_tensor("y_inA", [DSL, L // 2], BF16)
    y_inB = nc.dram_tensor("y_inB", [DSL, L // 2], BF16)
    y_gA = nc.dram_tensor("y_gA", [DI, L // 2], BF16)
    y_gB = nc.dram_tensor("y_gB", [DI, L // 2], BF16)
    u_pri = nc.dram_tensor("u_pri", [NC * DSL, TB], BF16)
    y_pri2 = nc.dram_tensor("y_pri2", [DI, 2 * TB], BF16)
    y_pri = nc.dram_tensor("y_pri", [DI, TB], BF16)
    zs_pri = nc.dram_tensor("zs_pri", [DI, TB], BF16)
    wgbf = nc.dram_tensor("wgbf", [24 * 8 * 128, 128], BF16)
    wubf = nc.dram_tensor("wubf", [24 * 8 * 128, 128], BF16)
    wdbf = nc.dram_tensor("wdbf", [8 * 24 * 128, 128], BF16)
    wibf = nc.dram_tensor("wibf", [32 * 8 * 128, 128], BF16)
    wobf = nc.dram_tensor("wobf", [8 * 16 * 128, 128], BF16)

    if DEBUG:
        dbg_u = nc.dram_tensor("dbg_u", [NC * DSL, TB], BF16,
                               kind="ExternalOutput")
        dbg_xp = nc.dram_tensor("dbg_xp", [NC * 80, TB], F32,
                                kind="ExternalOutput")
        dbg_y2 = nc.dram_tensor("dbg_y2", [DI, 2 * TB], BF16,
                                kind="ExternalOutput")
        dbg_yin = nc.dram_tensor("dbg_yin", [DSL, L], BF16,
                                 kind="ExternalOutput")
        dbg_g = nc.dram_tensor("dbg_g", [128, 8 * TB], BF16,
                               kind="ExternalOutput")
        dbg_zs = nc.dram_tensor("dbg_zs", [DI, TB], BF16,
                                kind="ExternalOutput")
        dbg_xcv = nc.dram_tensor("dbg_xcv", [DI, 4 * 512], BF16,
                                 kind="ExternalOutput")

    sch = Sched()

    with ExitStack() as gctx:
        SBg = lambda n, s, d: gctx.enter_context(nc.sbuf_tensor(n, s, d))
        DX = Ctr(gctx.enter_context(nc.semaphore("DX")))
        DG = Ctr(gctx.enter_context(nc.semaphore("DG")))
        TM = Ctr(gctx.enter_context(nc.semaphore("TM")))
        SV = Ctr(gctx.enter_context(nc.semaphore("SV")))
        VV = Ctr(gctx.enter_context(nc.semaphore("VV")))
        CC = Ctr(gctx.enter_context(nc.semaphore("CC")))
        DY = Ctr(gctx.enter_context(nc.semaphore("DY")))

        quad0 = gctx.enter_context(nc.psum_tensor("quad0", [128, 2048], F32))
        fps = gctx.enter_context(nc.psum_tensor("fps", [128, 1024], F32))
        pBC = gctx.enter_context(nc.psum_tensor("pBC", [128, 1024], F32))

        cw_sb = SBg("cw_sb", [128, 16, 4], F32)
        cb_sb = SBg("cb_sb", [128, 16], F32)
        dp_sb = SBg("dp_sb", [128, 16], F32)
        wdt_sb = SBg("wdt_sb", [128, 2], F32)
        bdt_sb = SBg("bdt_sb", [128, 2], F32)
        an_sb = SBg("an_sb", [128, 32], F32)
        onec_sb = SBg("onec_sb", [128, 1], F32)
        oner_sb = SBg("oner_sb", [1, 128], F32)
        id_f = SBg("id_f", [128, 128], F32)
        id_bf = SBg("id_bf", [128, 128], BF16)
        oneh_f = SBg("oneh_f", [16, 16 * 128], F32)
        oneh_bf = SBg("oneh_bf", [16, 16 * 128], BF16)
        cv_sb = SBg("cv_sb", [1, 1], I32)
        yo_sb = SBg("yo_sb", [1, 3], I32)
        alog_st = SBg("alog_st", [128, 2, DS], F32)
        wx_bf = SBg("wx_bf", [128, 16 * 80], BF16)
        eps_sb = SBg("eps_sb", [128, 1], F32)
        g_sb = SBg("g_sb", [128, 8 * TB], BF16)

        def dma(e, out_ap, in_ap):
            return e.dma_start(out=out_ap, in_=in_ap)

        small = [
            (cw_sb[:, :, :], cwt.ap().rearrange("(a p) k -> p a k", p=128)),
            (onec_sb[:, :], ones_c[:, :]),
            (oner_sb[:, :], ones_r[:, :]),
            (id_f[:, :], ident[:, :]),
            (oneh_f[:, :], oneh[:, :]),
            (cv_sb[:, :], cvec[:, :]),
            (yo_sb[:, :], yoff[:, :]),
            (alog_st[:, :, :], alog.ap().rearrange("(a p) n -> p a n", p=128)),
        ]
        for dst, src in small:
            sch.add("sync", lambda e, d=dst, s=src: dma(e, d, s), inc=(DX, 16))
        for colsb, coldr, na in ((cb_sb, cbt, 16), (dp_sb, dpar, 16),
                                 (wdt_sb, wdt, 2), (bdt_sb, bdt, 2)):
            for a in range(na):
                sch.add("sync", lambda e, d=colsb[:, a:a + 1],
                        s=coldr.ap()[a * 128:(a + 1) * 128, :]:
                        dma(e, d, s), inc=(DX, 16))
        init_dx = DX.n

        sch.add("vector", lambda e: e.tensor_copy(id_bf[:, :], id_f[:, :]),
                waits=[(DX, init_dx)])
        sch.add("vector", lambda e: e.tensor_copy(oneh_bf[:, :], oneh_f[:, :]))
        sch.add("vector", lambda e: e.memset(eps_sb[:, :], LNEPS), inc=(VV, 1))
        sch.add("scalar", lambda e: e.activation(
            an_sb[:, :], alog_st[:, :, :].rearrange("p a n -> p (a n)"),
            AF.Exp), waits=[(DX, init_dx)])
        sch.add("scalar", lambda e: e.mul(an_sb[:, :], an_sb[:, :], -1.0),
                inc=(SV, 1))

        ictx = ExitStack()
        if True:
            SBi = lambda n, s, d: ictx.enter_context(nc.sbuf_tensor(n, s, d))
            qi = [SBi(f"qi{i}", [128, 2048], I32) for i in range(2)]
            qw = [SBi(f"qw{i}", [128, 2048], F32) for i in range(2)]
            qf = [SBi(f"qf{i}", [128, 2048], BF16) for i in range(2)]
            wx_f = SBi("wx_f", [128, 16 * 80], F32)
            sch.add("sync", lambda e: dma(
                e, wx_f[:, :].rearrange("p (a k) -> p a k", k=80),
                wxt.ap().rearrange("(a p) k -> p a k", p=128)),
                inc=(DX, 16))
            wx_dx = DX.n
            sch.add("vector", lambda e: e.tensor_copy(wx_bf[:, :], wx_f[:, :]),
                    waits=[(DX, wx_dx)], inc=(VV, 1))

            steps = []
            for wsrc, wdst, nblk, isint in (
                    (wg_t, wgbf, 12, True), (wu_t, wubf, 12, True),
                    (wd_t, wdbf, 12, True), (wit, wibf, 16, False),
                    (wot, wobf, 8, False)):
                for blk in range(nblk):
                    steps.append((wsrc, wdst, blk, isint))
            dq_store = {}
            dq_cast = {}
            for i, (wsrc, wdst, blk, isint) in enumerate(steps):
                b = i % 2
                src = wsrc.ap()[ds(blk * 2048, 2048), :].rearrange(
                    "(a p) c -> p a c", p=128)
                dst = wdst.ap()[ds(blk * 2048, 2048), :].rearrange(
                    "(a p) c -> p a c", p=128)
                w_in = [(VV, dq_cast.get(i - 2, 0))]
                buf = qi[b] if isint else qw[b]
                sch.add("sync", lambda e, s=src, bu=buf: dma(
                    e, bu[:, :].rearrange("p (a c) -> p a c", c=128), s),
                    waits=w_in, inc=(DX, 16))
                ld = DX.n
                if isint:
                    sch.add("vector", lambda e, b=b: e.tensor_scalar(
                        qf[b][:, :], qi[b][:, :], QSCALE, None, ALU.mult),
                        waits=[(DX, ld), (DG, dq_store.get(i - 2, 0))],
                        inc=(VV, 1))
                else:
                    sch.add("vector", lambda e, b=b: e.tensor_copy(
                        qf[b][:, :], qw[b][:, :]),
                        waits=[(DX, ld), (DG, dq_store.get(i - 2, 0))],
                        inc=(VV, 1))
                dq_cast[i] = VV.n
                sch.add("gpsimd", lambda e, d=dst, b=b: dma(
                    e, d, qf[b][:, :].rearrange("p (a c) -> p a c", c=128)),
                    waits=[(VV, dq_cast[i])], inc=(DG, 16))
                dq_store[i] = DG.n
            dequant_dg = DG.n
            last_dq_cast = dq_cast[len(steps) - 1]
            ictx.close()

            xnctx = ExitStack()
            xn_sb = xnctx.enter_context(
                nc.sbuf_tensor("xn_sb", [128, 8 * TA], BF16))
            with ExitStack() as actx:
                SBa = lambda n, s, d: actx.enter_context(nc.sbuf_tensor(n, s, d))
                xt_sb = SBa("xt_sb", [128, 8 * 512], F32)
                xsq = SBa("xsq", [128, 8 * 512], F32)
                mrow = SBa("mrow", [1, 512], F32)
                vrow = SBa("vrow", [1, 512], F32)
                rrow = SBa("rrow", [1, 512], F32)
                tmp_f = SBa("tmp_f", [128, 512], F32)
                wib_sb = [SBa(f"wib_sb{i}", [128, 8 * 128], BF16)
                          for i in range(2)]
                xin_sb = [SBa(f"xin_sb{i}", [128, 16 * 512], BF16)
                          for i in range(2)]
                xcv_sb = [SBa(f"xcv_sb{i}", [128, 16 * 512], BF16)
                          for i in range(2)]
                zs_tmp = [SBa(f"zs_tmp{i}", [128, 512], BF16)
                          for i in range(6)]
                xe_sb = SBa("xe_sb", [128, 520], BF16)
                acc_sb = [SBa(f"acc_sb{i}", [128, 512], F32) for i in range(2)]
                tail_sb = SBa("tail_sb", [128, 16 * 3], BF16)
                xp_sb = [SBa(f"xp_sb{i}", [80, 512], F32) for i in range(2)]

                sch.add("vector", lambda e: e.memset(tail_sb[:, :], 0.0))

                sumsq_tm = {}
                sq_sv_d = {}
                evac_sv = {}
                wimm_tm = {}
                conv_vv = {}
                silu_sv = {}
                xnlast_vv = {}
                convlast_vv = {}
                xp_sv_d = {}
                wx_tm_d = {}
                zs_dg = {}
                xcvp_dg = {}
                xp_dg = {}
                conv_sv_d = {}
                x_loaded = {}
                ag_cc = {}
                for ci, (c0, w) in enumerate(ACH):
                    xb = ci % 2
                    src = xt.ap()[:, c0:c0 + w].rearrange(
                        "(a p) t -> p a t", p=128)
                    dst3 = xt_sb[:, :].rearrange(
                        "p (a t) -> p a t", t=512)[:, :, 0:w]
                    xtw = [(VV, xnlast_vv.get(ci - 1, 0)),
                           (SV, sq_sv_d.get(ci - 1, 0))]
                    if ci == 0:
                        xtw += [(VV, last_dq_cast), (DG, dequant_dg)]
                    sch.add("sync", lambda e, s=src, d=dst3: dma(e, d, s),
                            waits=xtw, inc=(DX, 16))
                    x_loaded[ci] = DX.n

                    for k in range(8):
                        sl = xt_sb[:, :].rearrange(
                            "p (a t) -> p a t", t=512)[:, k, 0:w]
                        wts = []
                        if k == 0:
                            wts = [(DX, max(x_loaded[ci], init_dx)),
                                   (SV, xp_sv_d.get(ci - 1, 0))]
                        sch.add("tensor", lambda e, sl=sl, k=k, w=w: e.matmul(
                            quad0[0:1, 0:w], onec_sb[:, :], sl,
                            start=(k == 0), stop=(k == 7)),
                            waits=wts, inc=(TM, 1) if k == 7 else None)
                    sch.add("scalar", lambda e: e.activation(
                        xsq[:, :], xt_sb[:, :], AF.Square),
                        waits=[(DX, x_loaded[ci]),
                               (TM, sumsq_tm.get(ci - 1, 0))], inc=(SV, 1))
                    sq_sv = SV.n
                    sq_sv_d[ci] = SV.n
                    for k in range(8):
                        sq = xsq[:, :].rearrange(
                            "p (a t) -> p a t", t=512)[:, k, 0:w]
                        sch.add("tensor", lambda e, sq=sq, k=k, w=w: e.matmul(
                            quad0[0:1, 512:512 + w], onec_sb[:, :], sq,
                            start=(k == 0), stop=(k == 7)),
                            waits=[(SV, sq_sv)] if k == 0 else [],
                            inc=(TM, 1) if k == 7 else None)
                    sumsq_tm[ci] = TM.n
                    sums_tm = TM.n

                    def ln_rows(e, w=w):
                        e.tensor_scalar(mrow[:, 0:w], quad0[0:1, 0:w],
                                        1.0 / DM, None, ALU.mult)
                        e.tensor_scalar(vrow[:, 0:w], quad0[0:1, 512:512 + w],
                                        1.0 / DM, None, ALU.mult)
                        e.tensor_mul(rrow[:, 0:w], mrow[:, 0:w], mrow[:, 0:w])
                        return e.tensor_sub(vrow[:, 0:w], vrow[:, 0:w],
                                            rrow[:, 0:w])
                    sch.add("vector", ln_rows, waits=[(TM, sums_tm)],
                            inc=(VV, 1))
                    var_vv = VV.n

                    def ln_rstd(e, w=w):
                        e.activation(rrow[:, 0:w], vrow[:, 0:w], AF.Ln,
                                     bias=eps_sb[0:1, 0:1])
                        return e.activation(rrow[:, 0:w], rrow[:, 0:w],
                                            AF.Exp, scale=-0.5)
                    sch.add("scalar", ln_rstd, waits=[(VV, var_vv)],
                            inc=(SV, 1))
                    rstd_sv = SV.n
                    sch.add("tensor", lambda e, w=w: e.matmul(
                        quad0[:, 1024:1024 + w], oner_sb[:, :], mrow[:, 0:w],
                        start=True, stop=True), waits=[(VV, var_vv)])
                    sch.add("tensor", lambda e, w=w: e.matmul(
                        quad0[:, 1536:1536 + w], oner_sb[:, :], rrow[:, 0:w],
                        start=True, stop=True), waits=[(SV, rstd_sv)],
                        inc=(TM, 1))
                    bc_tm = TM.n

                    for k in range(8):
                        sl = xt_sb[:, :].rearrange(
                            "p (a t) -> p a t", t=512)[:, k, 0:w]
                        xn_slice = xn_sb[:, k * TA + c0:k * TA + c0 + w]

                        def xn_ops(e, sl=sl, xn_slice=xn_slice, w=w):
                            e.tensor_sub(tmp_f[:, 0:w], sl,
                                         quad0[:, 1024:1024 + w])
                            return e.tensor_mul(xn_slice, tmp_f[:, 0:w],
                                                quad0[:, 1536:1536 + w])
                        sch.add("vector", xn_ops,
                                waits=[(TM, bc_tm)] if k == 0 else [],
                                inc=(VV, 1) if k == 7 else None)
                    xnlast_vv[ci] = VV.n
                    xn_vv = VV.n

                    for otl in range(32):
                        wb = otl % 2
                        wsrc = wibf.ap()[ds(otl * 1024, 1024), :].rearrange(
                            "(a p) c -> p a c", p=128)
                        prev_mm = wimm_tm.get((ci, otl - 2), 0) or \
                            wimm_tm.get((ci - 1, otl + 30), 0)
                        sch.add("sync", lambda e, s=wsrc, wb=wb: dma(
                            e, wib_sb[wb][:, :].rearrange(
                                "p (a c) -> p a c", c=128), s),
                            waits=[(TM, prev_mm)], inc=(DX, 16))
                        wld = DX.n
                        slot = otl % 2
                        ps = fps[:, slot * 512:slot * 512 + w]
                        prev_ev = evac_sv.get((ci, otl - 1), 0) or \
                            evac_sv.get((ci - 1, 31), 0)
                        for k in range(8):
                            rhs = xn_sb[:, k * TA + c0:k * TA + c0 + w]
                            wts = []
                            if k == 0:
                                wts = [(VV, xn_vv), (DX, wld), (SV, prev_ev)]
                            sch.add("tensor", lambda e, ps=ps, wb=wb, k=k,
                                    rhs=rhs: e.matmul(
                                        ps,
                                        wib_sb[wb][:, k * 128:(k + 1) * 128],
                                        rhs, start=(k == 0), stop=(k == 7)),
                                    waits=wts,
                                    inc=(TM, 1) if k == 7 else None)
                        wimm_tm[(ci, otl)] = TM.n
                        mm_tm = TM.n
                        if otl < 16:
                            dstt = xin_sb[xb][:, otl * 512:otl * 512 + w]
                            wts = [(TM, mm_tm)]
                            if otl == 0:
                                wts.append((VV, convlast_vv.get(ci - 2, 0)))
                            sch.add("scalar", lambda e, ps=ps, d=dstt: e.copy(
                                d, ps), waits=wts, inc=(SV, 1))
                        else:
                            z = (otl - 16) % 6
                            pz = (ci, otl - 6) if otl - 6 >= 16 else \
                                (ci - 1, 28 + z if z <= 3 else 22 + z)
                            wts = [(TM, mm_tm), (DG, zs_dg.get(pz, 0))]
                            sch.add("scalar", lambda e, ps=ps, zb=z, w=w:
                                    e.activation(zs_tmp[zb][:, 0:w], ps,
                                                 AF.Silu),
                                    waits=wts, inc=(SV, 1))
                        evac_sv[(ci, otl)] = SV.n
                        if otl >= 20:
                            dz = otl - 4
                            v0 = max(c0, EX)
                            off = v0 - c0
                            vw = c0 + w - v0
                            g0 = v0 - EX
                            sch.add("gpsimd", lambda e, zb=(dz - 16) % 6,
                                    off=off, vw=vw, g0=g0,
                                    r0=(dz - 16) * 128: dma(
                                        e, zs_pri.ap()[r0:r0 + 128,
                                                       g0:g0 + vw],
                                        zs_tmp[zb][:, off:off + vw]),
                                    waits=[(SV, evac_sv[(ci, otl)])],
                                    inc=(DG, 16))
                            zs_dg[(ci, dz)] = DG.n
                    xz_sv = SV.n

                    for dt16 in range(16):
                        def conv_ops(e, dt16=dt16, w=w, ab=dt16 % 2, xb=xb):
                            e.tensor_copy(xe_sb[:, 0:3],
                                          tail_sb[:, dt16 * 3:dt16 * 3 + 3])
                            e.tensor_copy(
                                xe_sb[:, 3:3 + w],
                                xin_sb[xb][:, dt16 * 512:dt16 * 512 + w])
                            e.tensor_copy(tail_sb[:, dt16 * 3:dt16 * 3 + 3],
                                          xin_sb[xb][:, dt16 * 512 + w - 3:
                                                     dt16 * 512 + w])
                            e.tensor_scalar(acc_sb[ab][:, 0:w], xe_sb[:, 0:w],
                                            cw_sb[:, dt16, 0:1], None,
                                            ALU.mult)
                            ins = None
                            for j in (1, 2, 3):
                                ins = e.scalar_tensor_tensor(
                                    acc_sb[ab][:, 0:w], xe_sb[:, j:j + w],
                                    cw_sb[:, dt16, j:j + 1],
                                    acc_sb[ab][:, 0:w], ALU.mult, ALU.add)
                            return ins
                        wts = [(SV, silu_sv.get((ci, dt16 - 2), 0) or
                                silu_sv.get((ci - 1, dt16 + 14), 0))]
                        if dt16 == 0:
                            wts.append((SV, xz_sv))
                        sch.add("vector", conv_ops, waits=wts, inc=(VV, 1))
                        conv_vv[(ci, dt16)] = VV.n
                        wts2 = [(VV, conv_vv[(ci, dt16)])]
                        if dt16 == 0:
                            wts2 += [(DG, xcvp_dg.get(ci - 2, 0)),
                                     (TM, wx_tm_d.get(ci - 2, 0))]
                        sch.add("scalar", lambda e, dt16=dt16, w=w,
                                ab=dt16 % 2, xb=xb: e.activation(
                                    xcv_sb[xb][:, dt16 * 512:dt16 * 512 + w],
                                    acc_sb[ab][:, 0:w], AF.Silu,
                                    bias=cb_sb[:, dt16:dt16 + 1]),
                                waits=wts2, inc=(SV, 1))
                        silu_sv[(ci, dt16)] = SV.n
                        if dt16 < 4:
                            dz = 28 + dt16
                            v0 = max(c0, EX)
                            off = v0 - c0
                            vw = c0 + w - v0
                            g0 = v0 - EX
                            sch.add("gpsimd", lambda e, zb=(dz - 16) % 6,
                                    off=off, vw=vw, g0=g0,
                                    r0=(dz - 16) * 128: dma(
                                        e, zs_pri.ap()[r0:r0 + 128,
                                                       g0:g0 + vw],
                                        zs_tmp[zb][:, off:off + vw]),
                                    waits=[(SV, silu_sv[(ci, dt16)])],
                                    inc=(DG, 16))
                            zs_dg[(ci, dz)] = DG.n
                    convlast_vv[ci] = conv_vv[(ci, 15)]
                    conv_sv = SV.n
                    conv_sv_d[ci] = SV.n

                    for k in range(16):
                        wts = []
                        if k == 0:
                            wts = [(SV, conv_sv),
                                   (SV, xp_sv_d.get(ci - 1, 0))]
                        sch.add("tensor", lambda e, k=k, w=w, xb=xb: e.matmul(
                            quad0[0:80, 0:w], wx_bf[:, k * 80:(k + 1) * 80],
                            xcv_sb[xb][:, k * 512:k * 512 + w],
                            start=(k == 0), stop=(k == 15)),
                            waits=wts, inc=(TM, 1) if k == 15 else None)
                    wx_tm = TM.n
                    wx_tm_d[ci] = TM.n
                    sch.add("scalar", lambda e, w=w, xb=xb: e.copy(
                        xp_sb[xb][:, 0:w], quad0[0:80, 0:w]),
                        waits=[(TM, wx_tm), (DG, xp_dg.get(ci - 2, 0))],
                        inc=(SV, 1))
                    xp_sv_d[ci] = SV.n

                    v0 = max(c0, EX)
                    off = v0 - c0
                    vw = c0 + w - v0
                    g0 = v0 - EX
                    sch.add("gpsimd", lambda e, off=off, vw=vw, g0=g0,
                            xb=xb: dma(
                                e, xp_in.ap()[:, g0:g0 + vw],
                                xp_sb[xb][:, off:off + vw]),
                            waits=[(SV, xp_sv_d[ci])], inc=(DG, 16))
                    xp_dg[ci] = DG.n

                    k = ci
                    src_p = xcv_sb[xb][:, :].rearrange(
                        "p (a t) -> p a t", t=512)[:, :, off:off + vw]
                    dst_p = xcv_p[k].ap()[:, :].rearrange(
                        "(a p) t -> p a t", p=128)
                    sch.add("gpsimd", lambda e, s=src_p, d=dst_p:
                            dma(e, d, s), waits=[(SV, conv_sv)],
                            inc=(DG, 16))
                    xcvp_dg[k] = DG.n

                pa_done_w = [(SV, SV.n), (VV, VV.n), (TM, TM.n),
                             (DG, DG.n), (DX, DX.n)]
                for k in range(5):
                    sch.add("gpsimd", lambda e, k=k: e.collective_compute(
                        "AllGather", ALU.bypass,
                        replica_groups=[list(range(NC))],
                        ins=[xcv_p[k].ap().opt()],
                        outs=[xcv_gk[k].ap().opt()]),
                        waits=[(DG, xcvp_dg[k])] +
                        (pa_done_w if k == 0 else []), inc=(CC, 1))
                    ag_cc[k] = CC.n

                    def u_slice(e, k=k):
                        reg = nc.alloc_register(e.engine, f"cid_g{k}")
                        e.reg_load(reg, cv_sb[0:1, 0:1])
                        cid = e.snap(reg, min_val=0, max_val=NC - 1)
                        s = xcv_gk[k].ap().rearrange(
                            "(rk d) t -> rk d t", rk=NC)[
                            :, ds(cid * DSL, DSL), :]
                        d = u_pri.ap().rearrange(
                            "(rk d) t -> rk d t", rk=NC)[
                            :, :, PC0[k]:PC0[k] + PCW[k]]
                        return dma(e, d, s)
                    sch.add("gpsimd", u_slice,
                            waits=[(CC, ag_cc[k]), (DY, DY.n)],
                            inc=(DY, 16))
                sch.add("gpsimd", lambda e: e.collective_compute(
                    "AllGather", ALU.bypass,
                    replica_groups=[list(range(NC))],
                    ins=[xp_in.ap().opt()], outs=[xp_g.ap().opt()]),
                    waits=[(DG, xp_dg[4])], inc=(CC, 1))
                xp_ag_cc = CC.n
                u_pri_dy = DY.n
                pa_vv, pa_sv, pa_tm, pa_dg = VV.n, SV.n, TM.n, DG.n
                if DEBUG:
                    sch.add("gpsimd", lambda e: dma(
                        e, dbg_u.ap()[:, :], u_pri.ap()[:, :]),
                        waits=[(DY, u_pri_dy)], inc=(DG, 16))
                    sch.add("gpsimd", lambda e: dma(
                        e, dbg_xp.ap()[:, :], xp_g.ap()[:, :]),
                        waits=[(CC, xp_ag_cc)], inc=(DG, 16))
                    sch.add("gpsimd", lambda e: dma(
                        e, dbg_zs.ap()[:, :], zs_pri.ap()[:, :]),
                        waits=[(DG, pa_dg)], inc=(DG, 16))
                    for k in range(5):
                        sch.add("gpsimd", lambda e, k=k: dma(
                            e, dbg_xcv.ap()[:, PC0[k]:PC0[k] + PCW[k]],
                            xcv_p[k].ap()[:, :]),
                            waits=[(DG, pa_dg)], inc=(DG, 16))

            with ExitStack() as mctx:
                SBm = lambda n, s, d: mctx.enter_context(nc.sbuf_tensor(n, s, d))
                hf_sb = SBm("hf_sb", [128, 24 * FPW], BF16)
                gs_sb = [SBm(f"gs_sb{i}", [128, FPW], BF16) for i in range(2)]
                up_sb = [SBm(f"up_sb{i}", [128, FPW], BF16) for i in range(2)]
                wg_sb = [SBm(f"wg_sb{i}", [128, 8 * 128], BF16)
                         for i in range(2)]
                wu_sb = [SBm(f"wu_sb{i}", [128, 8 * 128], BF16)
                         for i in range(2)]
                wdn_sb = [SBm(f"wdn_sb{i}", [128, 24 * 128], BF16)
                          for i in range(2)]
                u_sb = [SBm(f"u_sb{i}", [128, SCT], BF16) for i in range(2)]
                dt_sb = [SBm(f"dt_sb{i}", [128, SCT], F32) for i in range(2)]
                w_sb = [SBm(f"w_sb{i}", [128, SCT], BF16) for i in range(2)]
                sp_sb = [SBm(f"sp_sb{i}", [128, SCT], F32) for i in range(2)]
                sp2_sb = [SBm(f"sp2_sb{i}", [128, SCT], F32) for i in range(2)]
                a_sb = [SBm(f"a_sb{i}", [128, SCT], BF16) for i in range(4)]
                be_sb = SBm("be_sb", [128, SCT], BF16)
                h_sb = SBm("h_sb", [128, SCT], BF16)
                hc_sb = [SBm(f"hc_sb{i}", [128, SCT], BF16) for i in range(2)]
                bb_sb = [SBm(f"bb_sb{i}", [128, SCT], BF16) for i in range(2)]
                cc_sb = [SBm(f"cc_sb{i}", [128, SCT], BF16) for i in range(2)]
                b16 = SBm("b16", [16, SCT], F32)
                c16 = SBm("c16", [16, SCT], F32)
                b16b = SBm("b16b", [16, SCT], BF16)
                c16b = SBm("c16b", [16, SCT], BF16)
                dtr_sb = SBm("dtr_sb", [1, SCT], F32)
                hcar = SBm("hcar", [128, 32], F32)
                ysl = [SBm(f"ysl{i}", [128, SCT], BF16) for i in range(2)]

                phase_w = [(TM, pa_tm), (SV, pa_sv), (VV, pa_vv),
                           (DG, pa_dg)]

                st = {"guev": {}, "hf_vv": {}, "dev": {}, "dmm": {},
                      "gumm": {}, "hf_fp_vv": {}}

                def sched_gu_unit(fp, ht):
                    hb = ht % 2
                    po = fp * FPW
                    first = (fp == 0 and ht == 0)
                    ww = [(SV, st["guev"].get((fp, ht - 2), 0) or
                           st["guev"].get((fp - 1, ht + 22), 0))]
                    if first:
                        ww = list(phase_w) + [(CC, xp_ag_cc)]
                    for warr, wsb in ((wgbf, wg_sb), (wubf, wu_sb)):
                        wsrc = warr.ap()[ds(ht * 1024, 1024), :].rearrange(
                            "(a p) c -> p a c", p=128)
                        sch.add("sync", lambda e, s=wsrc, d=wsb[hb]: dma(
                            e, d[:, :].rearrange("p (a c) -> p a c", c=128),
                            s), waits=ww, inc=(DX, 16))
                        ww = []
                    wld = DX.n
                    if ht > 0:
                        pw = [(SV, st["guev"][(fp, ht - 1)])]
                    else:
                        pw = [(SV, st["dev"].get((fp - 1, 7), 0))]
                    for k in range(8):
                        rhs = xn_sb[:, k * TA + EX + po:
                                    k * TA + EX + po + FPW]
                        sch.add("tensor", lambda e, k=k, rhs=rhs, hb=hb:
                                e.matmul(fps[:, 0:512],
                                         wg_sb[hb][:, k * 128:(k + 1) * 128],
                                         rhs, start=(k == 0), stop=(k == 7)),
                                waits=([(DX, wld)] + pw) if k == 0 else [])
                        sch.add("tensor", lambda e, k=k, rhs=rhs, hb=hb:
                                e.matmul(fps[:, 512:1024],
                                         wu_sb[hb][:, k * 128:(k + 1) * 128],
                                         rhs, start=(k == 0), stop=(k == 7)),
                                inc=(TM, 1) if k == 7 else None)
                    mm_tm = TM.n
                    st["gumm"][(fp, ht)] = mm_tm
                    hw = [(TM, mm_tm),
                          (VV, st["hf_vv"].get((fp, ht - 2), 0) or
                           st["hf_vv"].get((fp - 1, ht + 22), 0))]

                    def gu_evac(e, hb=hb):
                        e.activation(gs_sb[hb][:, :], fps[:, 0:512], AF.Silu)
                        return e.copy(up_sb[hb][:, :], fps[:, 512:1024])
                    sch.add("scalar", gu_evac, waits=hw, inc=(SV, 1))
                    st["guev"][(fp, ht)] = SV.n
                    hfw = [(SV, st["guev"][(fp, ht)])]
                    if ht == 0:
                        hfw.append((TM, st["dmm"].get((fp - 1, 7), 0)))
                    sch.add("vector", lambda e, hb=hb, ht=ht: e.tensor_mul(
                        hf_sb[:, ht * FPW:(ht + 1) * FPW], gs_sb[hb][:, :],
                        up_sb[hb][:, :]), waits=hfw, inc=(VV, 1))
                    st["hf_vv"][(fp, ht)] = VV.n
                    if ht == 23:
                        st["hf_fp_vv"][fp] = VV.n

                def sched_dn_unit(fp, mt):
                    mb = mt % 2
                    po = fp * FPW
                    wsrc = wdbf.ap()[ds(mt * 3072, 3072), :].rearrange(
                        "(a p) c -> p a c", p=128)
                    ww = [(TM, st["dmm"].get((fp, mt - 2), 0) or
                           st["dmm"].get((fp - 1, mt + 6), 0))]
                    sch.add("sync", lambda e, s=wsrc, d=wdn_sb[mb]: dma(
                        e, d[:, :].rearrange("p (a c) -> p a c", c=128), s),
                        waits=ww, inc=(DX, 16))
                    wld = DX.n
                    if mt > 0:
                        pw = [(SV, st["dev"][(fp, mt - 1)])]
                    else:
                        pw = [(SV, st["guev"][(fp, 23)])]
                    for k in range(24):
                        sch.add("tensor", lambda e, k=k, mb=mb: e.matmul(
                            fps[:, 0:512],
                            wdn_sb[mb][:, k * 128:(k + 1) * 128],
                            hf_sb[:, k * FPW:(k + 1) * FPW],
                            start=(k == 0), stop=(k == 23)),
                            waits=([(DX, wld), (VV, st["hf_fp_vv"][fp])] + pw)
                            if k == 0 else [],
                            inc=(TM, 1) if k == 23 else None)
                    st["dmm"][(fp, mt)] = TM.n
                    sch.add("scalar", lambda e, mt=mt, po=po: e.activation(
                        g_sb[:, mt * TB + po:mt * TB + po + FPW],
                        fps[:, 0:512], AF.Silu),
                        waits=[(TM, st["dmm"][(fp, mt)])], inc=(SV, 1))
                    st["dev"][(fp, mt)] = SV.n

                ffn_units = []
                for fp in range(NFP):
                    for ht in range(24):
                        ffn_units.append(("gu", fp, ht))
                    for mt in range(8):
                        ffn_units.append(("dn", fp, mt))
                ffn_pos = [0]

                def emit_ffn(nmax):
                    n = 0
                    while n < nmax and ffn_pos[0] < len(ffn_units):
                        kind, a, b = ffn_units[ffn_pos[0]]
                        if kind == "gu":
                            sched_gu_unit(a, b)
                        else:
                            sched_dn_unit(a, b)
                        ffn_pos[0] += 1
                        n += 1

                sch.add("vector", lambda e: e.memset(hcar[:, :], 0.0),
                        waits=phase_w + [(CC, xp_ag_cc)], inc=(VV, 1))

                yev_vv = {}
                ydma_dg = {}
                scan_vv_d = {}
                ns_tm_d = {}
                dtb_s_sv = {}
                dtb_x_vv = {}
                dtrb_sv_d = {}
                dtr_tm_d = {}
                cev_sv = {}
                bev_sv = {}
                bmm_d = {}
                cmm_d = {}
                alast_sv = {}
                agy_cc = {}
                yslice_dg = {}

                for ch in range(NSC):
                    rk, half = ch // 2, ch % 2
                    colr = half * SCT
                    part = ch // 8
                    pcol = ch * SCT - part * (L // 2)
                    y_dst = y_inA if part == 0 else y_inB
                    for dtl in range(2):
                        def u_load(e, rk=rk, dtl=dtl, colr=colr):
                            row0 = rk * DSL + dtl * 128
                            return dma(e, u_sb[dtl][:, :],
                                       u_pri.ap()[row0:row0 + 128,
                                                  colr:colr + SCT])
                        uw = [(VV, yev_vv.get(ch - 1, 0))]
                        if ch == 0:
                            uw = list(phase_w) + [(DY, u_pri_dy)]
                        sch.add("sync", u_load, waits=uw, inc=(DX, 16))
                    u_ld = DX.n
                    bcw = [(CC, xp_ag_cc)] if ch == 0 else []
                    sch.add("sync", lambda e, rk=rk, colr=colr: dma(
                        e, b16[:, :],
                        xp_g.ap()[rk * 80:rk * 80 + 16, colr:colr + SCT]),
                        waits=bcw, inc=(DX, 16))
                    sch.add("sync", lambda e, rk=rk, colr=colr: dma(
                        e, c16[:, :],
                        xp_g.ap()[rk * 80 + 32:rk * 80 + 48, colr:colr + SCT]),
                        inc=(DX, 16))
                    sch.add("sync", lambda e, rk=rk, colr=colr: dma(
                        e, dtr_sb[:, :],
                        xp_g.ap()[rk * 80 + 64:rk * 80 + 65, colr:colr + SCT]),
                        waits=[(TM, dtr_tm_d.get(ch - 1, 0))], inc=(DX, 16))
                    xp_ld = DX.n
                    sch.add("vector", lambda e: e.tensor_copy(
                        b16b[:, :], b16[:, :]),
                        waits=[(DX, xp_ld), (TM, cmm_d.get((ch - 1, 15), 0))])
                    sch.add("vector", lambda e: e.tensor_copy(
                        c16b[:, :], c16[:, :]), inc=(VV, 1))
                    bc16_vv = VV.n

                    dtw = [(DX, xp_ld), (SV, cev_sv.get((ch - 1, 15), 0)),
                           (VV, dtb_x_vv.get((ch - 1, 1), 0))]
                    sch.add("tensor", lambda e: e.matmul(
                        pBC[:, 0:512], oner_sb[:, :], dtr_sb[:, 0:512],
                        start=True, stop=True), waits=dtw)
                    sch.add("tensor", lambda e: e.matmul(
                        pBC[:, 512:1024], oner_sb[:, :], dtr_sb[:, 512:1024],
                        start=True, stop=True), inc=(TM, 1))
                    dtr_tm = TM.n
                    dtr_tm_d[ch] = TM.n
                    for dtl in range(2):
                        def dt_x(e, dtl=dtl):
                            return e.tensor_scalar(
                                dt_sb[dtl][:, :], pBC[:, :],
                                wdt_sb[:, dtl:dtl + 1],
                                bdt_sb[:, dtl:dtl + 1], ALU.mult, ALU.add)
                        sch.add("vector", dt_x,
                                waits=[(TM, dtr_tm),
                                       (SV, alast_sv.get(ch - 1, 0))],
                                inc=(VV, 1))
                        dtb_x_vv[(ch, dtl)] = VV.n

                        def dt_s(e, dtl=dtl):
                            e.activation(sp2_sb[dtl][:, :], dt_sb[dtl][:, :],
                                         AF.Abs)
                            e.activation(sp2_sb[dtl][:, :], sp2_sb[dtl][:, :],
                                         AF.Exp, scale=-1.0)
                            return e.activation(sp2_sb[dtl][:, :],
                                                sp2_sb[dtl][:, :], AF.Ln,
                                                bias=1.0)
                        sch.add("scalar", dt_s,
                                waits=[(VV, dtb_x_vv[(ch, dtl)])],
                                inc=(SV, 1))
                        dtb_s_sv[(ch, dtl)] = SV.n

                        def dt_v(e, dtl=dtl):
                            e.tensor_relu(sp_sb[dtl][:, :], dt_sb[dtl][:, :])
                            e.tensor_add(dt_sb[dtl][:, :], sp_sb[dtl][:, :],
                                         sp2_sb[dtl][:, :])
                            return e.tensor_mul(w_sb[dtl][:, :],
                                                dt_sb[dtl][:, :],
                                                u_sb[dtl][:, :])
                        sch.add("vector", dt_v,
                                waits=[(SV, dtb_s_sv[(ch, dtl)]), (DX, u_ld)],
                                inc=(VV, 1))
                    w_vv = VV.n

                    for n in range(DS):
                        nb = n % 2
                        bw = [(SV, cev_sv.get((ch, n - 1), 0) or
                               cev_sv.get((ch - 1, 15), 0))]
                        if n == 0:
                            bw += [(VV, dtb_x_vv[(ch, 1)]), (VV, bc16_vv)]
                        sch.add("tensor", lambda e, n=n: e.matmul(
                            pBC[:, 0:512], oneh_bf[:, n * 128:(n + 1) * 128],
                            b16b[:, 0:512], start=True, stop=True),
                            waits=bw)
                        sch.add("tensor", lambda e, n=n: e.matmul(
                            pBC[:, 512:1024],
                            oneh_bf[:, n * 128:(n + 1) * 128],
                            b16b[:, 512:1024], start=True, stop=True),
                            inc=(TM, 1))
                        bmm = TM.n
                        bmm_d[(ch, n)] = TM.n
                        sch.add("scalar", lambda e, nb=nb: e.copy(
                            bb_sb[nb][:, :], pBC[:, :]),
                            waits=[(TM, bmm),
                                   (VV, scan_vv_d.get((ch, n - 2, 1), 0) or
                                    scan_vv_d.get((ch - 1, 15, 1), 0))],
                            inc=(SV, 1))
                        bev_sv[(ch, n)] = SV.n
                        sch.add("tensor", lambda e, n=n: e.matmul(
                            pBC[:, 0:512], oneh_bf[:, n * 128:(n + 1) * 128],
                            c16b[:, 0:512], start=True, stop=True),
                            waits=[(SV, bev_sv[(ch, n)])])
                        sch.add("tensor", lambda e, n=n: e.matmul(
                            pBC[:, 512:1024],
                            oneh_bf[:, n * 128:(n + 1) * 128],
                            c16b[:, 512:1024], start=True, stop=True),
                            inc=(TM, 1))
                        cmm = TM.n
                        cmm_d[(ch, n)] = TM.n
                        sch.add("scalar", lambda e, nb=nb: e.copy(
                            cc_sb[nb][:, :], pBC[:, :]),
                            waits=[(TM, cmm),
                                   (VV, scan_vv_d.get((ch, n - 2, 1), 0) or
                                    scan_vv_d.get((ch - 1, 15, 1), 0))],
                            inc=(SV, 1))
                        cev_sv[(ch, n)] = SV.n
                        for dtl in range(2):
                            idx = dtl * 16 + n
                            ab = (2 * (n % 2)) + dtl
                            sch.add("scalar", lambda e, dtl=dtl, idx=idx,
                                    ab=ab: e.activation(
                                        a_sb[ab][:, :], dt_sb[dtl][:, :],
                                        AF.Exp,
                                        scale=an_sb[:, idx:idx + 1]),
                                    waits=[(VV, w_vv),
                                           (VV, scan_vv_d.get(
                                               (ch, n - 2, dtl), 0) or
                                            scan_vv_d.get(
                                                (ch - 1, 15, dtl), 0))],
                                    inc=(SV, 1))
                            a_sv = SV.n

                            def scan_ops(e, dtl=dtl, idx=idx, nb=nb, ab=ab,
                                         ch=ch):
                                e.tensor_mul(be_sb[:, :], w_sb[dtl][:, :],
                                             bb_sb[nb][:, :])
                                e.tensor_tensor_scan(
                                    h_sb[:, :], a_sb[ab][:, :], be_sb[:, :],
                                    0.0 if ch == 0 else hcar[:, idx:idx + 1],
                                    ALU.mult, ALU.add)
                                e.tensor_mul(hc_sb[dtl][:, :], h_sb[:, :],
                                             cc_sb[nb][:, :])
                                return e.tensor_copy(hcar[:, idx:idx + 1],
                                                     h_sb[:, SCT - 1:SCT])
                            prev_ns = ns_tm_d.get((ch, n - 1, dtl), 0) or \
                                ns_tm_d.get((ch - 1, 15, dtl), 0)
                            sch.add("vector", scan_ops,
                                    waits=[(SV, a_sv), (SV, bev_sv[(ch, n)]),
                                           (SV, cev_sv[(ch, n)]),
                                           (TM, prev_ns),
                                           (VV, scan_vv_d.get(
                                               (ch - 1, n, dtl), 0))],
                                    inc=(VV, 1))
                            scan_vv_d[(ch, n, dtl)] = VV.n
                            for sub in range(2):
                                wts = []
                                if sub == 0:
                                    wts = [(VV, scan_vv_d[(ch, n, dtl)])]
                                    if n == 0 and dtl == 0:
                                        wts.append(
                                            (VV, yev_vv.get(ch - 1, 0)))
                                sch.add("tensor", lambda e, dtl=dtl, sub=sub,
                                        n=n: e.matmul(
                                            quad0[:, dtl * 1024 + sub * 512:
                                                  dtl * 1024 + sub * 512 + 512],
                                            id_bf[:, :],
                                            hc_sb[dtl][:, sub * 512:
                                                       (sub + 1) * 512],
                                            start=(n == 0),
                                            stop=(n == DS - 1)),
                                        waits=wts,
                                        inc=(TM, 1) if sub == 1 else None)
                            ns_tm_d[(ch, n, dtl)] = TM.n
                        if n in (3, 7, 11, 15):
                            emit_ffn(2)
                    alast_sv[ch] = SV.n
                    ns_tm = TM.n

                    for dtl in range(2):
                        sch.add("vector", lambda e, dtl=dtl: e.tensor_copy(
                            ysl[dtl][:, :],
                            quad0[:, dtl * 1024:(dtl + 1) * 1024]),
                            waits=[(TM, ns_tm),
                                   (DG, ydma_dg.get(ch - 1, 0))],
                            inc=(VV, 1))
                        yv = VV.n
                        sch.add("gpsimd", lambda e, dtl=dtl, y_dst=y_dst,
                                pcol=pcol: dma(
                                    e, y_dst.ap()[dtl * 128:dtl * 128 + 128,
                                                  pcol:pcol + SCT],
                                    ysl[dtl][:, :]),
                                waits=[(VV, yv)], inc=(DG, 16))
                    yev_vv[ch] = VV.n
                    ydma_dg[ch] = DG.n

                emit_ffn(len(ffn_units))
                assert ffn_pos[0] == len(ffn_units)

                mid_done_w = [(DG, DG.n), (SV, SV.n), (VV, VV.n),
                              (TM, TM.n), (DX, DX.n)]
                for p, (yi, yg) in enumerate(((y_inA, y_gA),
                                              (y_inB, y_gB))):
                    sch.add("gpsimd", lambda e, yi=yi, yg=yg:
                            e.collective_compute(
                                "AllGather", ALU.bypass,
                                replica_groups=[list(range(NC))],
                                ins=[yi.ap().opt()],
                                outs=[yg.ap().opt()]),
                            waits=mid_done_w if p == 0 else [], inc=(CC, 1))
                    agy_cc[p] = CC.n

                def y_slice_a(e):
                    reg = nc.alloc_register(e.engine, "yca_g")
                    e.reg_load(reg, yo_sb[0:1, 0:1])
                    yca = e.snap(reg, min_val=0, max_val=L // 2 - TB)
                    return dma(e, y_pri2.ap()[:, 0:TB],
                               y_gA.ap()[:, ds(yca, TB)])
                sch.add("gpsimd", y_slice_a,
                        waits=[(CC, agy_cc[0]), (DY, DY.n)], inc=(DY, 16))

                def y_slice_b(e):
                    reg = nc.alloc_register(e.engine, "ycb_g")
                    e.reg_load(reg, yo_sb[0:1, 1:2])
                    ycb = e.snap(reg, min_val=0, max_val=L // 2 - TB)
                    return dma(e, y_pri2.ap()[:, TB:2 * TB],
                               y_gB.ap()[:, ds(ycb, TB)])
                sch.add("gpsimd", y_slice_b,
                        waits=[(CC, agy_cc[1]), (DY, DY.n)], inc=(DY, 16))

                def y_sel(e):
                    reg = nc.alloc_register(e.engine, "ysel_g")
                    e.reg_load(reg, yo_sb[0:1, 2:3])
                    ysel = e.snap(reg, min_val=0, max_val=TB)
                    return dma(e, y_pri.ap()[:, :],
                               y_pri2.ap()[:, ds(ysel, TB)])
                sch.add("gpsimd", y_sel, waits=[(DY, DY.n)], inc=(DY, 16))
                yslice_dg[1] = DY.n
                if DEBUG:
                    sch.add("gpsimd", lambda e: dma(
                        e, dbg_y2.ap()[:, :], y_pri2.ap()[:, :]),
                        waits=[(DY, yslice_dg[1])], inc=(DG, 16))
                    sch.add("gpsimd", lambda e: dma(
                        e, dbg_yin.ap()[:, 0:L // 2], y_inA.ap()[:, :]),
                        waits=[(DG, ydma_dg[7])], inc=(DG, 16))
                    sch.add("gpsimd", lambda e: dma(
                        e, dbg_yin.ap()[:, L // 2:L], y_inB.ap()[:, :]),
                        waits=[(DG, ydma_dg[15])], inc=(DG, 16))
                    sch.add("gpsimd", lambda e: dma(
                        e, dbg_g.ap()[:, :].rearrange(
                            "p (a t) -> p a t", t=TB),
                        g_sb[:, :].rearrange("p (a t) -> p a t", t=TB)),
                        waits=[(SV, SV.n)], inc=(DG, 16))
                mid_vv, mid_sv, mid_tm, mid_dx = VV.n, SV.n, TM.n, DX.n
            xnctx.close()

            with ExitStack() as ectx:
                SBe = lambda n, s, d: ectx.enter_context(nc.sbuf_tensor(n, s, d))
                yt_sb = SBe("yt_sb", [128, 16 * 512], BF16)
                xc2_sb = SBe("xc2_sb", [128, 16 * 512], BF16)
                zs2_sb = SBe("zs2_sb", [128, 16 * 512], BF16)
                y3_sb = [SBe(f"y3_sb{i}", [128, 16 * 512], BF16)
                         for i in range(2)]
                xt2_sb = [SBe(f"xt2_sb{i}", [128, 8 * 512], F32)
                          for i in range(2)]
                wob_sb = [SBe(f"wob_sb{i}", [128, 16 * 128], BF16)
                          for i in range(2)]
                p_sb = SBe("p_sb", [128, 512], BF16)
                oth_sb = SBe("oth_sb", [128, 8 * 512], F32)

                pe_phase_w = [(VV, mid_vv), (SV, mid_sv), (TM, mid_tm),
                              (DX, mid_dx), (DY, yslice_dg[1])]
                wo_mm_tm = {}
                y3_vv_d = {}
                pout_vv = {}
                pout_last = {}
                out_dg = {}
                yt_dg = {}
                ld_dx = {}

                pending_out = []

                def emit_out(sub):
                    c0 = sub * 512
                    sch.add("gpsimd", lambda e, c0=c0: dma(
                        e, ot.ap()[:, c0:c0 + 512].rearrange(
                            "(a p) t -> p a t", p=128),
                        oth_sb[:, :].rearrange("p (a t) -> p a t", t=512)),
                        waits=[(VV, pout_last[sub])], inc=(DG, 16))
                    out_dg[sub] = DG.n

                for sub in range(4):
                    c0 = sub * 512
                    eb = sub % 2

                    ylw = [(VV, y3_vv_d.get(sub - 1, 0))]
                    if sub == 0:
                        ylw = list(pe_phase_w)
                    sch.add("sync", lambda e, c0=c0: dma(
                        e, yt_sb[:, :].rearrange("p (a t) -> p a t", t=512),
                        y_pri.ap()[:, c0:c0 + 512].rearrange(
                            "(a p) t -> p a t", p=128)),
                        waits=ylw, inc=(DX, 16))
                    yt_dg[sub] = DX.n
                    if sub >= 1:
                        emit_out(sub - 1)

                    slw = []
                    if sub == 0:
                        parts = [(0, 0, 448), (1, 0, 64)]
                    else:
                        parts = [(sub, 64, 448), (sub + 1, 0, 64)]
                    dof = 0
                    for (pk, so, sw) in parts:
                        sch.add("sync", lambda e, pk=pk, so=so, sw=sw,
                                dof=dof: dma(
                                    e, xc2_sb[:, :].rearrange(
                                        "p (a t) -> p a t",
                                        t=512)[:, :, dof:dof + sw],
                                    xcv_p[pk].ap()[:, so:so + sw].rearrange(
                                        "(a p) t -> p a t", p=128)),
                                waits=slw, inc=(DX, 16))
                        slw = []
                        dof += sw
                    sch.add("sync", lambda e, c0=c0: dma(
                        e, zs2_sb[:, :].rearrange("p (a t) -> p a t", t=512),
                        zs_pri.ap()[:, c0:c0 + 512].rearrange(
                            "(a p) t -> p a t", p=128)), inc=(DX, 16))
                    sch.add("sync", lambda e, c0=c0, eb=eb: dma(
                        e, xt2_sb[eb][:, :].rearrange(
                            "p (a t) -> p a t", t=512),
                        xt.ap()[:, EX + c0:EX + c0 + 512].rearrange(
                            "(a p) t -> p a t", p=128)), inc=(DX, 16))
                    ld_dx[sub] = DX.n

                    for dt16 in range(16):
                        def comb(e, dt16=dt16, eb=eb):
                            e.scalar_tensor_tensor(
                                y3_sb[eb][:, dt16 * 512:(dt16 + 1) * 512],
                                xc2_sb[:, dt16 * 512:(dt16 + 1) * 512],
                                dp_sb[:, dt16:dt16 + 1],
                                yt_sb[:, dt16 * 512:(dt16 + 1) * 512],
                                ALU.mult, ALU.add)
                            return e.tensor_mul(
                                y3_sb[eb][:, dt16 * 512:(dt16 + 1) * 512],
                                y3_sb[eb][:, dt16 * 512:(dt16 + 1) * 512],
                                zs2_sb[:, dt16 * 512:(dt16 + 1) * 512])
                        wts = []
                        if dt16 == 0:
                            wts = [(DX, ld_dx[sub]),
                                   (TM, wo_mm_tm.get((sub - 2, 7), 0))]
                        sch.add("vector", comb, waits=wts,
                                inc=(VV, 1) if dt16 == 15 else None)
                    y3_vv_d[sub] = VV.n

                    for mt in range(8):
                        mb = mt % 2
                        wsrc = wobf.ap()[ds(mt * 2048, 2048), :].rearrange(
                            "(a p) c -> p a c", p=128)
                        ww = [(TM, wo_mm_tm.get((sub, mt - 2), 0) or
                               wo_mm_tm.get((sub - 1, mt + 6), 0))]
                        sch.add("sync", lambda e, s=wsrc, mb=mb: dma(
                            e, wob_sb[mb][:, :].rearrange(
                                "p (a c) -> p a c", c=128), s),
                            waits=ww, inc=(DX, 16))
                        wld = DX.n
                        slot = mt % 4
                        ps = quad0[:, slot * 512:(slot + 1) * 512]
                        pw = [(VV, pout_vv.get((sub, mt - 4), 0) or
                               pout_vv.get((sub - 1, mt + 4), 0))]
                        for k in range(16):
                            wts = []
                            if k == 0:
                                wts = [(VV, y3_vv_d[sub]), (DX, wld)] + pw
                            sch.add("tensor", lambda e, ps=ps, mb=mb, k=k,
                                    eb=eb: e.matmul(
                                        ps,
                                        wob_sb[mb][:, k * 128:(k + 1) * 128],
                                        y3_sb[eb][:, k * 512:(k + 1) * 512],
                                        start=(k == 0), stop=(k == 15)),
                                    waits=wts,
                                    inc=(TM, 1) if k == 15 else None)
                        wo_mm_tm[(sub, mt)] = TM.n

                        def pout(e, ps=ps, mt=mt, sub=sub, eb=eb, c0=c0):
                            e.tensor_mul(p_sb[:, :], ps,
                                         g_sb[:, mt * TB + c0:
                                              mt * TB + c0 + 512])
                            return e.tensor_add(
                                oth_sb[:, mt * 512:(mt + 1) * 512],
                                xt2_sb[eb][:, mt * 512:(mt + 1) * 512],
                                p_sb[:, :])
                        wts = [(TM, wo_mm_tm[(sub, mt)])]
                        if mt == 0:
                            wts.append((DG, out_dg.get(sub - 1, 0)))
                        sch.add("vector", pout, waits=wts, inc=(VV, 1))
                        pout_vv[(sub, mt)] = VV.n
                    pout_last[sub] = VV.n

                emit_out(3)
                sch.add("gpsimd", lambda e: e.engine_nop(),
                        waits=[(DG, DG.n)])

        with nc.Block() as block:
            @block.sync
            def _(eng):
                sch.emit(eng, "sync")

            @block.gpsimd
            def _(eng):
                sch.emit(eng, "gpsimd")

            @block.vector
            def _(eng):
                sch.emit(eng, "vector")

            @block.scalar
            def _(eng):
                sch.emit(eng, "scalar")

            @block.tensor
            def _(eng):
                sch.emit(eng, "tensor")

    return nc


_NC_CACHE = [None]
_LAST_RES = [None]


def kernel(L0_attn_x, L0_attn_W_in, L0_attn_conv_w, L0_attn_conv_b,
           L0_attn_W_x, L0_attn_W_dt, L0_attn_b_dt, L0_attn_A_log,
           L0_attn_D_param, L0_attn_W_out, L0_ffn_w_gate, L0_ffn_w_up,
           L0_ffn_w_down):
    x = np.asarray(L0_attn_x)
    xT = np.ascontiguousarray(x.reshape(L, DM).T)
    wiT = np.ascontiguousarray(np.asarray(L0_attn_W_in).T)
    wit_t = np.ascontiguousarray(
        wiT.reshape(8, 128, 32, 128).transpose(2, 0, 1, 3)).reshape(-1, 128)
    wx = np.asarray(L0_attn_W_x)
    wxt = np.zeros((DI, 80), np.float32)
    wxt[:, 0:16] = wx[0:16, :].T
    wxt[:, 32:48] = wx[16:32, :].T
    wxt[:, 64:65] = wx[32:33, :].T
    cw = np.ascontiguousarray(np.asarray(L0_attn_conv_w).reshape(DI, 4))
    cb = np.ascontiguousarray(np.asarray(L0_attn_conv_b).reshape(DI, 1))
    dparam = np.ascontiguousarray(np.asarray(L0_attn_D_param).reshape(DI, 1))
    wdt_full = np.asarray(L0_attn_W_dt).reshape(DI, 1)
    bdt_full = np.asarray(L0_attn_b_dt).reshape(DI, 1)
    alog_full = np.asarray(L0_attn_A_log)
    wg = np.asarray(L0_ffn_w_gate)
    wu = np.asarray(L0_ffn_w_up)
    wd = np.asarray(L0_ffn_w_down)
    wg_t = np.ascontiguousarray(
        wg.T.reshape(8, 128, 24, 128).transpose(2, 0, 1, 3)).reshape(-1, 128)
    wu_t = np.ascontiguousarray(
        wu.T.reshape(8, 128, 24, 128).transpose(2, 0, 1, 3)).reshape(-1, 128)
    wd_t = np.ascontiguousarray(
        wd.T.reshape(24, 128, 8, 128).transpose(2, 0, 1, 3)).reshape(-1, 128)
    woT = np.ascontiguousarray(np.asarray(L0_attn_W_out).T)
    wot_t = np.ascontiguousarray(
        woT.reshape(16, 128, 8, 128).transpose(2, 0, 1, 3)).reshape(-1, 128)
    ones_cv = np.ones((128, 1), np.float32)
    ones_rv = np.ones((1, 128), np.float32)
    identv = np.eye(128, dtype=np.float32)
    onehv = np.zeros((16, 16 * 128), np.float32)
    for n in range(16):
        onehv[n, n * 128:(n + 1) * 128] = 1.0

    if _NC_CACHE[0] is None:
        _NC_CACHE[0] = build_nc()
    nc = _NC_CACHE[0]

    in_maps = []
    for c in range(NC):
        xt_c = np.zeros((DM, TA), np.float32)
        t0 = c * TB
        xt_c[:, EX:] = xT[:, t0:t0 + TB]
        lo = max(0, t0 - EX)
        n_pre = t0 - lo
        if n_pre > 0:
            xt_c[:, EX - n_pre:EX] = xT[:, lo:t0]
        dsl = slice(c * DSL, (c + 1) * DSL)
        yca = c * TB if c < 4 else 0
        ycb = (c - 4) * TB if c >= 4 else 0
        ysel = 0 if c < 4 else TB
        in_maps.append({
            "xt": xt_c,
            "wit": wit_t,
            "wxt": wxt,
            "cwt": cw, "cbt": cb, "dpar": dparam,
            "wdt": np.ascontiguousarray(wdt_full[dsl]),
            "bdt": np.ascontiguousarray(bdt_full[dsl]),
            "alog": np.ascontiguousarray(alog_full[dsl]),
            "wg_t": wg_t, "wu_t": wu_t, "wd_t": wd_t,
            "wot": wot_t,
            "ones_c": ones_cv, "ones_r": ones_rv, "ident": identv,
            "oneh": onehv,
            "cvec": np.array([[c]], np.int32),
            "yoff": np.array([[yca, ycb, ysel]], np.int32),
        })

    res = run_bass_kernel_spmd(nc, in_maps, core_ids=list(range(NC)))
    _LAST_RES[0] = res
    out = np.empty((L, DM), np.float32)
    for c in range(NC):
        out[c * TB:(c + 1) * TB, :] = np.asarray(res.results[c]["ot"]).T
    return out.reshape(1, L, DM)


# version 05
# speedup vs baseline: 1.2886x; 1.2886x over previous
import numpy as np
from contextlib import ExitStack

import concourse.bass as bass
import concourse.mybir as mybir
from concourse.bass_utils import run_bass_kernel_spmd

F32 = mybir.dt.float32
BF16 = mybir.dt.bfloat16
I32 = mybir.dt.int32
ALU = mybir.AluOpType
AF = mybir.ActivationFunctionType
ds = bass.ds

NC = 8
DM = 1024
DI = 2048
DS = 16
HID = 3072
L = 16384
TB = L // NC
EX = 64
TA = TB + EX
DSL = DI // NC
LNEPS = 1e-5
QSCALE = 1.0 / 127.0

ACH = [(0, 512), (512, 512), (1024, 512), (1536, 512), (2048, 64)]
SCT = 1024
NSC = L // SCT
FPW = 512
NFP = TB // FPW

DEBUG = False


class Ctr:
    def __init__(self, h):
        self.h = h
        self.n = 0

    def bump(self, amt):
        self.n += amt
        return self.n


class Sched:
    ENGS = ("sync", "gpsimd", "vector", "scalar", "tensor")

    def __init__(self):
        self.ops = {e: [] for e in self.ENGS}

    def add(self, eng, fn, waits=(), inc=None):
        waits = [(c, v) for (c, v) in waits if v > 0]
        for c, v in waits:
            assert v <= c.n, f"wait on future count {v} > {c.n}"
        entry = [fn, waits, None]
        if inc is not None:
            c, amt = inc
            c.bump(amt)
            entry[2] = (c, amt)
        self.ops[eng].append(entry)

    def emit(self, eng_obj, eng_name):
        for fn, waits, inc in self.ops[eng_name]:
            for c, v in waits:
                eng_obj.wait_ge(c.h, v)
            ins = fn(eng_obj)
            if inc is not None:
                c, amt = inc
                assert ins is not None
                ins.then_inc(c.h, amt)


def build_nc():
    nc = bass.Bass("TRN2", target_bir_lowering=False,
                   detect_race_conditions=False)

    xt = nc.dram_tensor("xt", [DM, TA], F32, kind="ExternalInput")
    wit = nc.dram_tensor("wit", [32 * 8 * 128, 128], F32, kind="ExternalInput")
    wxt = nc.dram_tensor("wxt", [DI, 80], F32, kind="ExternalInput")
    cwt = nc.dram_tensor("cwt", [DI, 4], F32, kind="ExternalInput")
    cbt = nc.dram_tensor("cbt", [DI, 1], F32, kind="ExternalInput")
    dpar = nc.dram_tensor("dpar", [DI, 1], F32, kind="ExternalInput")
    wdt = nc.dram_tensor("wdt", [DSL, 1], F32, kind="ExternalInput")
    bdt = nc.dram_tensor("bdt", [DSL, 1], F32, kind="ExternalInput")
    alog = nc.dram_tensor("alog", [DSL, DS], F32, kind="ExternalInput")
    wg_t = nc.dram_tensor("wg_t", [24 * 8 * 128, 128], I32, kind="ExternalInput")
    wu_t = nc.dram_tensor("wu_t", [24 * 8 * 128, 128], I32, kind="ExternalInput")
    wd_t = nc.dram_tensor("wd_t", [8 * 24 * 128, 128], I32, kind="ExternalInput")
    wot = nc.dram_tensor("wot", [8 * 16 * 128, 128], F32, kind="ExternalInput")
    ones_c = nc.dram_tensor("ones_c", [128, 1], F32, kind="ExternalInput")
    ones_r = nc.dram_tensor("ones_r", [1, 128], F32, kind="ExternalInput")
    ident = nc.dram_tensor("ident", [128, 128], F32, kind="ExternalInput")
    oneh = nc.dram_tensor("oneh", [16, 16 * 128], F32, kind="ExternalInput")
    cvec = nc.dram_tensor("cvec", [1, 1], I32, kind="ExternalInput")
    yoff = nc.dram_tensor("yoff", [1, 3], I32, kind="ExternalInput")
    ot = nc.dram_tensor("ot", [DM, TB], F32, kind="ExternalOutput")

    PCW = [448, 512, 512, 512, 64]
    PC0 = [0, 448, 960, 1472, 1984]
    xcv_p = [nc.dram_tensor(f"xcv_p{k}", [DI, PCW[k]], BF16)
             for k in range(5)]
    xcv_gk = [nc.dram_tensor(f"xcv_g{k}", [NC * DI, PCW[k]], BF16)
              for k in range(5)]
    xp_in = nc.dram_tensor("xp_in", [80, TB], F32)
    xp_g = nc.dram_tensor("xp_g", [NC * 80, TB], F32)
    y_inA = nc.dram_tensor("y_inA", [DSL, L // 2], BF16)
    y_inB = nc.dram_tensor("y_inB", [DSL, L // 2], BF16)
    y_gA = nc.dram_tensor("y_gA", [DI, L // 2], BF16)
    y_gB = nc.dram_tensor("y_gB", [DI, L // 2], BF16)
    u_pri = nc.dram_tensor("u_pri", [NC * DSL, TB], BF16)
    y_pri2 = nc.dram_tensor("y_pri2", [DI, 2 * TB], BF16)
    y_pri = nc.dram_tensor("y_pri", [DI, TB], BF16)
    zs_pri = nc.dram_tensor("zs_pri", [DI, TB], BF16)
    wgbf = nc.dram_tensor("wgbf", [24 * 8 * 128, 128], BF16)
    wubf = nc.dram_tensor("wubf", [24 * 8 * 128, 128], BF16)
    wdbf = nc.dram_tensor("wdbf", [8 * 24 * 128, 128], BF16)
    wibf = nc.dram_tensor("wibf", [32 * 8 * 128, 128], BF16)
    wobf = nc.dram_tensor("wobf", [8 * 16 * 128, 128], BF16)

    if DEBUG:
        dbg_u = nc.dram_tensor("dbg_u", [NC * DSL, TB], BF16,
                               kind="ExternalOutput")
        dbg_xp = nc.dram_tensor("dbg_xp", [NC * 80, TB], F32,
                                kind="ExternalOutput")
        dbg_y2 = nc.dram_tensor("dbg_y2", [DI, 2 * TB], BF16,
                                kind="ExternalOutput")
        dbg_yin = nc.dram_tensor("dbg_yin", [DSL, L], BF16,
                                 kind="ExternalOutput")
        dbg_g = nc.dram_tensor("dbg_g", [128, 8 * TB], BF16,
                               kind="ExternalOutput")
        dbg_zs = nc.dram_tensor("dbg_zs", [DI, TB], BF16,
                                kind="ExternalOutput")
        dbg_xcv = nc.dram_tensor("dbg_xcv", [DI, 4 * 512], BF16,
                                 kind="ExternalOutput")

    sch = Sched()

    with ExitStack() as gctx:
        SBg = lambda n, s, d: gctx.enter_context(nc.sbuf_tensor(n, s, d))
        DX = Ctr(gctx.enter_context(nc.semaphore("DX")))
        DG = Ctr(gctx.enter_context(nc.semaphore("DG")))
        TM = Ctr(gctx.enter_context(nc.semaphore("TM")))
        SV = Ctr(gctx.enter_context(nc.semaphore("SV")))
        VV = Ctr(gctx.enter_context(nc.semaphore("VV")))
        CC = Ctr(gctx.enter_context(nc.semaphore("CC")))
        DY = Ctr(gctx.enter_context(nc.semaphore("DY")))
        GP = Ctr(gctx.enter_context(nc.semaphore("GP")))

        quad0 = gctx.enter_context(nc.psum_tensor("quad0", [128, 2048], F32))
        fps = gctx.enter_context(nc.psum_tensor("fps", [128, 1024], F32))

        cw_sb = SBg("cw_sb", [128, 16, 4], F32)
        cb_sb = SBg("cb_sb", [128, 16], F32)
        dp_sb = SBg("dp_sb", [128, 16], F32)
        wdt_sb = SBg("wdt_sb", [128, 2], F32)
        bdt_sb = SBg("bdt_sb", [128, 2], F32)
        an_sb = SBg("an_sb", [128, 32], F32)
        onec_sb = SBg("onec_sb", [128, 1], F32)
        oner_sb = SBg("oner_sb", [1, 128], F32)
        id_f = SBg("id_f", [128, 128], F32)
        id_bf = SBg("id_bf", [128, 128], BF16)
        oneh_f = SBg("oneh_f", [16, 16 * 128], F32)
        oneh_bf = SBg("oneh_bf", [16, 16 * 128], BF16)
        cv_sb = SBg("cv_sb", [1, 1], I32)
        yo_sb = SBg("yo_sb", [1, 3], I32)
        alog_st = SBg("alog_st", [128, 2, DS], F32)
        wx_bf = SBg("wx_bf", [128, 16 * 80], BF16)
        eps_sb = SBg("eps_sb", [128, 1], F32)
        g_sb = SBg("g_sb", [128, 8 * TB], BF16)

        def dma(e, out_ap, in_ap):
            return e.dma_start(out=out_ap, in_=in_ap)

        small = [
            (cw_sb[:, :, :], cwt.ap().rearrange("(a p) k -> p a k", p=128)),
            (onec_sb[:, :], ones_c[:, :]),
            (oner_sb[:, :], ones_r[:, :]),
            (id_f[:, :], ident[:, :]),
            (oneh_f[:, :], oneh[:, :]),
            (cv_sb[:, :], cvec[:, :]),
            (yo_sb[:, :], yoff[:, :]),
            (alog_st[:, :, :], alog.ap().rearrange("(a p) n -> p a n", p=128)),
        ]
        for dst, src in small:
            sch.add("sync", lambda e, d=dst, s=src: dma(e, d, s), inc=(DX, 16))
        for colsb, coldr, na in ((cb_sb, cbt, 16), (dp_sb, dpar, 16),
                                 (wdt_sb, wdt, 2), (bdt_sb, bdt, 2)):
            for a in range(na):
                sch.add("sync", lambda e, d=colsb[:, a:a + 1],
                        s=coldr.ap()[a * 128:(a + 1) * 128, :]:
                        dma(e, d, s), inc=(DX, 16))
        init_dx = DX.n

        sch.add("vector", lambda e: e.tensor_copy(id_bf[:, :], id_f[:, :]),
                waits=[(DX, init_dx)])
        sch.add("vector", lambda e: e.tensor_copy(oneh_bf[:, :], oneh_f[:, :]))
        sch.add("vector", lambda e: e.memset(eps_sb[:, :], LNEPS), inc=(VV, 1))
        sch.add("scalar", lambda e: e.activation(
            an_sb[:, :], alog_st[:, :, :].rearrange("p a n -> p (a n)"),
            AF.Exp), waits=[(DX, init_dx)])
        sch.add("scalar", lambda e: e.mul(an_sb[:, :], an_sb[:, :], -1.0),
                inc=(SV, 1))

        ictx = ExitStack()
        if True:
            SBi = lambda n, s, d: ictx.enter_context(nc.sbuf_tensor(n, s, d))
            qi = [SBi(f"qi{i}", [128, 2048], I32) for i in range(2)]
            qw = [SBi(f"qw{i}", [128, 2048], F32) for i in range(2)]
            qf = [SBi(f"qf{i}", [128, 2048], BF16) for i in range(2)]
            wx_f = SBi("wx_f", [128, 16 * 80], F32)
            sch.add("sync", lambda e: dma(
                e, wx_f[:, :].rearrange("p (a k) -> p a k", k=80),
                wxt.ap().rearrange("(a p) k -> p a k", p=128)),
                inc=(DX, 16))
            wx_dx = DX.n
            sch.add("vector", lambda e: e.tensor_copy(wx_bf[:, :], wx_f[:, :]),
                    waits=[(DX, wx_dx)], inc=(VV, 1))

            steps = []
            for wsrc, wdst, nblk, isint in (
                    (wg_t, wgbf, 12, True), (wu_t, wubf, 12, True),
                    (wd_t, wdbf, 12, True), (wit, wibf, 16, False),
                    (wot, wobf, 8, False)):
                for blk in range(nblk):
                    steps.append((wsrc, wdst, blk, isint))
            dq_store = {}
            dq_cast = {}
            for i, (wsrc, wdst, blk, isint) in enumerate(steps):
                b = i % 2
                src = wsrc.ap()[ds(blk * 2048, 2048), :].rearrange(
                    "(a p) c -> p a c", p=128)
                dst = wdst.ap()[ds(blk * 2048, 2048), :].rearrange(
                    "(a p) c -> p a c", p=128)
                w_in = [(VV, dq_cast.get(i - 2, 0))]
                buf = qi[b] if isint else qw[b]
                sch.add("sync", lambda e, s=src, bu=buf: dma(
                    e, bu[:, :].rearrange("p (a c) -> p a c", c=128), s),
                    waits=w_in, inc=(DX, 16))
                ld = DX.n
                if isint:
                    sch.add("vector", lambda e, b=b: e.tensor_scalar(
                        qf[b][:, :], qi[b][:, :], QSCALE, None, ALU.mult),
                        waits=[(DX, ld), (DG, dq_store.get(i - 2, 0))],
                        inc=(VV, 1))
                else:
                    sch.add("vector", lambda e, b=b: e.tensor_copy(
                        qf[b][:, :], qw[b][:, :]),
                        waits=[(DX, ld), (DG, dq_store.get(i - 2, 0))],
                        inc=(VV, 1))
                dq_cast[i] = VV.n
                sch.add("gpsimd", lambda e, d=dst, b=b: dma(
                    e, d, qf[b][:, :].rearrange("p (a c) -> p a c", c=128)),
                    waits=[(VV, dq_cast[i])], inc=(DG, 16))
                dq_store[i] = DG.n
            dequant_dg = DG.n
            last_dq_cast = dq_cast[len(steps) - 1]
            ictx.close()

            xnctx = ExitStack()
            xn_sb = xnctx.enter_context(
                nc.sbuf_tensor("xn_sb", [128, 8 * TA], BF16))
            with ExitStack() as actx:
                SBa = lambda n, s, d: actx.enter_context(nc.sbuf_tensor(n, s, d))
                xt_sb = SBa("xt_sb", [128, 8 * 512], F32)
                xsq = SBa("xsq", [128, 8 * 512], F32)
                mrow = SBa("mrow", [1, 512], F32)
                vrow = SBa("vrow", [1, 512], F32)
                rrow = SBa("rrow", [1, 512], F32)
                tmp_f = SBa("tmp_f", [128, 512], F32)
                wib_sb = [SBa(f"wib_sb{i}", [128, 8 * 128], BF16)
                          for i in range(2)]
                xin_sb = [SBa(f"xin_sb{i}", [128, 16 * 512], BF16)
                          for i in range(2)]
                xcv_sb = [SBa(f"xcv_sb{i}", [128, 16 * 512], BF16)
                          for i in range(2)]
                zs_tmp = [SBa(f"zs_tmp{i}", [128, 512], BF16)
                          for i in range(6)]
                xe_sb = SBa("xe_sb", [128, 520], BF16)
                acc_sb = [SBa(f"acc_sb{i}", [128, 512], F32) for i in range(2)]
                tail_sb = SBa("tail_sb", [128, 16 * 3], BF16)
                xp_sb = [SBa(f"xp_sb{i}", [80, 512], F32) for i in range(2)]

                sch.add("vector", lambda e: e.memset(tail_sb[:, :], 0.0))

                sumsq_tm = {}
                sq_sv_d = {}
                evac_sv = {}
                wimm_tm = {}
                conv_vv = {}
                silu_sv = {}
                xnlast_vv = {}
                convlast_vv = {}
                xp_sv_d = {}
                wx_tm_d = {}
                zs_dg = {}
                xcvp_dg = {}
                xp_dg = {}
                conv_sv_d = {}
                x_loaded = {}
                ag_cc = {}
                for ci, (c0, w) in enumerate(ACH):
                    xb = ci % 2
                    src = xt.ap()[:, c0:c0 + w].rearrange(
                        "(a p) t -> p a t", p=128)
                    dst3 = xt_sb[:, :].rearrange(
                        "p (a t) -> p a t", t=512)[:, :, 0:w]
                    xtw = [(VV, xnlast_vv.get(ci - 1, 0)),
                           (SV, sq_sv_d.get(ci - 1, 0))]
                    if ci == 0:
                        xtw += [(VV, last_dq_cast), (DG, dequant_dg)]
                    sch.add("sync", lambda e, s=src, d=dst3: dma(e, d, s),
                            waits=xtw, inc=(DX, 16))
                    x_loaded[ci] = DX.n

                    for k in range(8):
                        sl = xt_sb[:, :].rearrange(
                            "p (a t) -> p a t", t=512)[:, k, 0:w]
                        wts = []
                        if k == 0:
                            wts = [(DX, max(x_loaded[ci], init_dx)),
                                   (SV, xp_sv_d.get(ci - 1, 0))]
                        sch.add("tensor", lambda e, sl=sl, k=k, w=w: e.matmul(
                            quad0[0:1, 0:w], onec_sb[:, :], sl,
                            start=(k == 0), stop=(k == 7)),
                            waits=wts, inc=(TM, 1) if k == 7 else None)
                    sch.add("scalar", lambda e: e.activation(
                        xsq[:, :], xt_sb[:, :], AF.Square),
                        waits=[(DX, x_loaded[ci]),
                               (TM, sumsq_tm.get(ci - 1, 0))], inc=(SV, 1))
                    sq_sv = SV.n
                    sq_sv_d[ci] = SV.n
                    for k in range(8):
                        sq = xsq[:, :].rearrange(
                            "p (a t) -> p a t", t=512)[:, k, 0:w]
                        sch.add("tensor", lambda e, sq=sq, k=k, w=w: e.matmul(
                            quad0[0:1, 512:512 + w], onec_sb[:, :], sq,
                            start=(k == 0), stop=(k == 7)),
                            waits=[(SV, sq_sv)] if k == 0 else [],
                            inc=(TM, 1) if k == 7 else None)
                    sumsq_tm[ci] = TM.n
                    sums_tm = TM.n

                    def ln_rows(e, w=w):
                        e.tensor_scalar(mrow[:, 0:w], quad0[0:1, 0:w],
                                        1.0 / DM, None, ALU.mult)
                        e.tensor_scalar(vrow[:, 0:w], quad0[0:1, 512:512 + w],
                                        1.0 / DM, None, ALU.mult)
                        e.tensor_mul(rrow[:, 0:w], mrow[:, 0:w], mrow[:, 0:w])
                        return e.tensor_sub(vrow[:, 0:w], vrow[:, 0:w],
                                            rrow[:, 0:w])
                    sch.add("vector", ln_rows, waits=[(TM, sums_tm)],
                            inc=(VV, 1))
                    var_vv = VV.n

                    def ln_rstd(e, w=w):
                        e.activation(rrow[:, 0:w], vrow[:, 0:w], AF.Ln,
                                     bias=eps_sb[0:1, 0:1])
                        return e.activation(rrow[:, 0:w], rrow[:, 0:w],
                                            AF.Exp, scale=-0.5)
                    sch.add("scalar", ln_rstd, waits=[(VV, var_vv)],
                            inc=(SV, 1))
                    rstd_sv = SV.n
                    sch.add("tensor", lambda e, w=w: e.matmul(
                        quad0[:, 1024:1024 + w], oner_sb[:, :], mrow[:, 0:w],
                        start=True, stop=True), waits=[(VV, var_vv)])
                    sch.add("tensor", lambda e, w=w: e.matmul(
                        quad0[:, 1536:1536 + w], oner_sb[:, :], rrow[:, 0:w],
                        start=True, stop=True), waits=[(SV, rstd_sv)],
                        inc=(TM, 1))
                    bc_tm = TM.n

                    for k in range(8):
                        sl = xt_sb[:, :].rearrange(
                            "p (a t) -> p a t", t=512)[:, k, 0:w]
                        xn_slice = xn_sb[:, k * TA + c0:k * TA + c0 + w]

                        def xn_ops(e, sl=sl, xn_slice=xn_slice, w=w):
                            e.tensor_sub(tmp_f[:, 0:w], sl,
                                         quad0[:, 1024:1024 + w])
                            return e.tensor_mul(xn_slice, tmp_f[:, 0:w],
                                                quad0[:, 1536:1536 + w])
                        sch.add("vector", xn_ops,
                                waits=[(TM, bc_tm)] if k == 0 else [],
                                inc=(VV, 1) if k == 7 else None)
                    xnlast_vv[ci] = VV.n
                    xn_vv = VV.n

                    for otl in range(32):
                        wb = otl % 2
                        wsrc = wibf.ap()[ds(otl * 1024, 1024), :].rearrange(
                            "(a p) c -> p a c", p=128)
                        prev_mm = wimm_tm.get((ci, otl - 2), 0) or \
                            wimm_tm.get((ci - 1, otl + 30), 0)
                        sch.add("sync", lambda e, s=wsrc, wb=wb: dma(
                            e, wib_sb[wb][:, :].rearrange(
                                "p (a c) -> p a c", c=128), s),
                            waits=[(TM, prev_mm)], inc=(DX, 16))
                        wld = DX.n
                        slot = otl % 2
                        ps = fps[:, slot * 512:slot * 512 + w]
                        prev_ev = evac_sv.get((ci, otl - 1), 0) or \
                            evac_sv.get((ci - 1, 31), 0)
                        for k in range(8):
                            rhs = xn_sb[:, k * TA + c0:k * TA + c0 + w]
                            wts = []
                            if k == 0:
                                wts = [(VV, xn_vv), (DX, wld), (SV, prev_ev)]
                            sch.add("tensor", lambda e, ps=ps, wb=wb, k=k,
                                    rhs=rhs: e.matmul(
                                        ps,
                                        wib_sb[wb][:, k * 128:(k + 1) * 128],
                                        rhs, start=(k == 0), stop=(k == 7)),
                                    waits=wts,
                                    inc=(TM, 1) if k == 7 else None)
                        wimm_tm[(ci, otl)] = TM.n
                        mm_tm = TM.n
                        if otl < 16:
                            dstt = xin_sb[xb][:, otl * 512:otl * 512 + w]
                            wts = [(TM, mm_tm)]
                            if otl == 0:
                                wts.append((VV, convlast_vv.get(ci - 2, 0)))
                            sch.add("scalar", lambda e, ps=ps, d=dstt: e.copy(
                                d, ps), waits=wts, inc=(SV, 1))
                        else:
                            z = (otl - 16) % 6
                            pz = (ci, otl - 6) if otl - 6 >= 16 else \
                                (ci - 1, 28 + z if z <= 3 else 22 + z)
                            wts = [(TM, mm_tm), (DG, zs_dg.get(pz, 0))]
                            sch.add("scalar", lambda e, ps=ps, zb=z, w=w:
                                    e.activation(zs_tmp[zb][:, 0:w], ps,
                                                 AF.Silu),
                                    waits=wts, inc=(SV, 1))
                        evac_sv[(ci, otl)] = SV.n
                        if otl >= 20:
                            dz = otl - 4
                            v0 = max(c0, EX)
                            off = v0 - c0
                            vw = c0 + w - v0
                            g0 = v0 - EX
                            sch.add("gpsimd", lambda e, zb=(dz - 16) % 6,
                                    off=off, vw=vw, g0=g0,
                                    r0=(dz - 16) * 128: dma(
                                        e, zs_pri.ap()[r0:r0 + 128,
                                                       g0:g0 + vw],
                                        zs_tmp[zb][:, off:off + vw]),
                                    waits=[(SV, evac_sv[(ci, otl)])],
                                    inc=(DG, 16))
                            zs_dg[(ci, dz)] = DG.n
                    xz_sv = SV.n

                    for dt16 in range(16):
                        def conv_ops(e, dt16=dt16, w=w, ab=dt16 % 2, xb=xb):
                            e.tensor_copy(xe_sb[:, 0:3],
                                          tail_sb[:, dt16 * 3:dt16 * 3 + 3])
                            e.tensor_copy(
                                xe_sb[:, 3:3 + w],
                                xin_sb[xb][:, dt16 * 512:dt16 * 512 + w])
                            e.tensor_copy(tail_sb[:, dt16 * 3:dt16 * 3 + 3],
                                          xin_sb[xb][:, dt16 * 512 + w - 3:
                                                     dt16 * 512 + w])
                            e.tensor_scalar(acc_sb[ab][:, 0:w], xe_sb[:, 0:w],
                                            cw_sb[:, dt16, 0:1], None,
                                            ALU.mult)
                            ins = None
                            for j in (1, 2, 3):
                                ins = e.scalar_tensor_tensor(
                                    acc_sb[ab][:, 0:w], xe_sb[:, j:j + w],
                                    cw_sb[:, dt16, j:j + 1],
                                    acc_sb[ab][:, 0:w], ALU.mult, ALU.add)
                            return ins
                        wts = [(SV, silu_sv.get((ci, dt16 - 2), 0) or
                                silu_sv.get((ci - 1, dt16 + 14), 0))]
                        if dt16 == 0:
                            wts.append((SV, xz_sv))
                        sch.add("vector", conv_ops, waits=wts, inc=(VV, 1))
                        conv_vv[(ci, dt16)] = VV.n
                        wts2 = [(VV, conv_vv[(ci, dt16)])]
                        if dt16 == 0:
                            wts2 += [(DG, xcvp_dg.get(ci - 2, 0)),
                                     (TM, wx_tm_d.get(ci - 2, 0))]
                        sch.add("scalar", lambda e, dt16=dt16, w=w,
                                ab=dt16 % 2, xb=xb: e.activation(
                                    xcv_sb[xb][:, dt16 * 512:dt16 * 512 + w],
                                    acc_sb[ab][:, 0:w], AF.Silu,
                                    bias=cb_sb[:, dt16:dt16 + 1]),
                                waits=wts2, inc=(SV, 1))
                        silu_sv[(ci, dt16)] = SV.n
                        if dt16 < 4:
                            dz = 28 + dt16
                            v0 = max(c0, EX)
                            off = v0 - c0
                            vw = c0 + w - v0
                            g0 = v0 - EX
                            sch.add("gpsimd", lambda e, zb=(dz - 16) % 6,
                                    off=off, vw=vw, g0=g0,
                                    r0=(dz - 16) * 128: dma(
                                        e, zs_pri.ap()[r0:r0 + 128,
                                                       g0:g0 + vw],
                                        zs_tmp[zb][:, off:off + vw]),
                                    waits=[(SV, silu_sv[(ci, dt16)])],
                                    inc=(DG, 16))
                            zs_dg[(ci, dz)] = DG.n
                    convlast_vv[ci] = conv_vv[(ci, 15)]
                    conv_sv = SV.n
                    conv_sv_d[ci] = SV.n

                    for k in range(16):
                        wts = []
                        if k == 0:
                            wts = [(SV, conv_sv),
                                   (SV, xp_sv_d.get(ci - 1, 0))]
                        sch.add("tensor", lambda e, k=k, w=w, xb=xb: e.matmul(
                            quad0[0:80, 0:w], wx_bf[:, k * 80:(k + 1) * 80],
                            xcv_sb[xb][:, k * 512:k * 512 + w],
                            start=(k == 0), stop=(k == 15)),
                            waits=wts, inc=(TM, 1) if k == 15 else None)
                    wx_tm = TM.n
                    wx_tm_d[ci] = TM.n
                    sch.add("scalar", lambda e, w=w, xb=xb: e.copy(
                        xp_sb[xb][:, 0:w], quad0[0:80, 0:w]),
                        waits=[(TM, wx_tm), (DG, xp_dg.get(ci - 2, 0))],
                        inc=(SV, 1))
                    xp_sv_d[ci] = SV.n

                    v0 = max(c0, EX)
                    off = v0 - c0
                    vw = c0 + w - v0
                    g0 = v0 - EX
                    sch.add("gpsimd", lambda e, off=off, vw=vw, g0=g0,
                            xb=xb: dma(
                                e, xp_in.ap()[:, g0:g0 + vw],
                                xp_sb[xb][:, off:off + vw]),
                            waits=[(SV, xp_sv_d[ci])], inc=(DG, 16))
                    xp_dg[ci] = DG.n

                    k = ci
                    src_p = xcv_sb[xb][:, :].rearrange(
                        "p (a t) -> p a t", t=512)[:, :, off:off + vw]
                    dst_p = xcv_p[k].ap()[:, :].rearrange(
                        "(a p) t -> p a t", p=128)
                    sch.add("gpsimd", lambda e, s=src_p, d=dst_p:
                            dma(e, d, s), waits=[(SV, conv_sv)],
                            inc=(DG, 16))
                    xcvp_dg[k] = DG.n

                pa_done_w = [(SV, SV.n), (VV, VV.n), (TM, TM.n),
                             (DG, DG.n), (DX, DX.n)]
                for k in range(5):
                    sch.add("gpsimd", lambda e, k=k: e.collective_compute(
                        "AllGather", ALU.bypass,
                        replica_groups=[list(range(NC))],
                        ins=[xcv_p[k].ap().opt()],
                        outs=[xcv_gk[k].ap().opt()]),
                        waits=[(DG, xcvp_dg[k])] +
                        (pa_done_w if k == 0 else []), inc=(CC, 1))
                    ag_cc[k] = CC.n

                    def u_slice(e, k=k):
                        reg = nc.alloc_register(e.engine, f"cid_g{k}")
                        e.reg_load(reg, cv_sb[0:1, 0:1])
                        cid = e.snap(reg, min_val=0, max_val=NC - 1)
                        s = xcv_gk[k].ap().rearrange(
                            "(rk d) t -> rk d t", rk=NC)[
                            :, ds(cid * DSL, DSL), :]
                        d = u_pri.ap().rearrange(
                            "(rk d) t -> rk d t", rk=NC)[
                            :, :, PC0[k]:PC0[k] + PCW[k]]
                        return dma(e, d, s)
                    sch.add("gpsimd", u_slice,
                            waits=[(CC, ag_cc[k]), (DY, DY.n)],
                            inc=(DY, 16))
                sch.add("gpsimd", lambda e: e.collective_compute(
                    "AllGather", ALU.bypass,
                    replica_groups=[list(range(NC))],
                    ins=[xp_in.ap().opt()], outs=[xp_g.ap().opt()]),
                    waits=[(DG, xp_dg[4])], inc=(CC, 1))
                xp_ag_cc = CC.n
                u_pri_dy = DY.n
                pa_vv, pa_sv, pa_tm, pa_dg = VV.n, SV.n, TM.n, DG.n
                if DEBUG:
                    sch.add("gpsimd", lambda e: dma(
                        e, dbg_u.ap()[:, :], u_pri.ap()[:, :]),
                        waits=[(DY, u_pri_dy)], inc=(DG, 16))
                    sch.add("gpsimd", lambda e: dma(
                        e, dbg_xp.ap()[:, :], xp_g.ap()[:, :]),
                        waits=[(CC, xp_ag_cc)], inc=(DG, 16))
                    sch.add("gpsimd", lambda e: dma(
                        e, dbg_zs.ap()[:, :], zs_pri.ap()[:, :]),
                        waits=[(DG, pa_dg)], inc=(DG, 16))
                    for k in range(5):
                        sch.add("gpsimd", lambda e, k=k: dma(
                            e, dbg_xcv.ap()[:, PC0[k]:PC0[k] + PCW[k]],
                            xcv_p[k].ap()[:, :]),
                            waits=[(DG, pa_dg)], inc=(DG, 16))

            with ExitStack() as mctx:
                SBm = lambda n, s, d: mctx.enter_context(nc.sbuf_tensor(n, s, d))
                hf_sb = SBm("hf_sb", [128, 24 * FPW], BF16)
                gs_sb = [SBm(f"gs_sb{i}", [128, FPW], BF16) for i in range(2)]
                up_sb = [SBm(f"up_sb{i}", [128, FPW], BF16) for i in range(2)]
                wg_sb = [SBm(f"wg_sb{i}", [128, 8 * 128], BF16)
                         for i in range(2)]
                wu_sb = [SBm(f"wu_sb{i}", [128, 8 * 128], BF16)
                         for i in range(2)]
                wdn_sb = [SBm(f"wdn_sb{i}", [128, 24 * 128], BF16)
                          for i in range(1)]
                y_acc = SBm("y_acc", [128, 2048], F32)
                u_sb = [SBm(f"u_sb{i}", [128, SCT], BF16) for i in range(2)]
                dt_sb = [SBm(f"dt_sb{i}", [128, SCT], F32) for i in range(2)]
                w_sb = [SBm(f"w_sb{i}", [128, SCT], BF16) for i in range(2)]
                sp_sb = [SBm(f"sp_sb{i}", [128, SCT], F32) for i in range(2)]
                sp2_sb = [SBm(f"sp2_sb{i}", [128, SCT], F32) for i in range(2)]
                a_sb = [SBm(f"a_sb{i}", [128, SCT], BF16) for i in range(4)]
                be_sb = SBm("be_sb", [128, SCT], BF16)
                h_sb = SBm("h_sb", [128, SCT], BF16)
                hc_sb = [SBm(f"hc_sb{i}", [128, SCT], BF16) for i in range(2)]
                bb_sb = [SBm(f"bb_sb{i}", [128, SCT], BF16) for i in range(2)]
                cc_sb = [SBm(f"cc_sb{i}", [128, SCT], BF16) for i in range(2)]
                b16 = SBm("b16", [16, SCT], F32)
                c16 = SBm("c16", [16, SCT], F32)
                b16b = SBm("b16b", [16, SCT], BF16)
                c16b = SBm("c16b", [16, SCT], BF16)
                dtr_sb = SBm("dtr_sb", [1, SCT], F32)
                hcar = SBm("hcar", [128, 32], F32)
                ysl = [SBm(f"ysl{i}", [128, SCT], BF16) for i in range(2)]

                phase_w = [(TM, pa_tm), (SV, pa_sv), (VV, pa_vv),
                           (DG, pa_dg)]

                st = {"guev": {}, "hf_vv": {}, "dev": {}, "dmm": {},
                      "gumm": {}, "hf_fp_vv": {}}

                def sched_gu_unit(fp, ht):
                    hb = ht % 2
                    po = fp * FPW
                    first = (fp == 0 and ht == 0)
                    ww = [(SV, st["guev"].get((fp, ht - 2), 0) or
                           st["guev"].get((fp - 1, ht + 22), 0))]
                    if first:
                        ww = list(phase_w) + [(CC, xp_ag_cc)]
                    for warr, wsb in ((wgbf, wg_sb), (wubf, wu_sb)):
                        wsrc = warr.ap()[ds(ht * 1024, 1024), :].rearrange(
                            "(a p) c -> p a c", p=128)
                        sch.add("sync", lambda e, s=wsrc, d=wsb[hb]: dma(
                            e, d[:, :].rearrange("p (a c) -> p a c", c=128),
                            s), waits=ww, inc=(DX, 16))
                        ww = []
                    wld = DX.n
                    if ht > 0:
                        pw = [(SV, st["guev"][(fp, ht - 1)])]
                    else:
                        pw = [(SV, st["dev"].get((fp - 1, 7), 0))]
                    for k in range(8):
                        rhs = xn_sb[:, k * TA + EX + po:
                                    k * TA + EX + po + FPW]
                        sch.add("tensor", lambda e, k=k, rhs=rhs, hb=hb:
                                e.matmul(fps[:, 0:512],
                                         wg_sb[hb][:, k * 128:(k + 1) * 128],
                                         rhs, start=(k == 0), stop=(k == 7)),
                                waits=([(DX, wld)] + pw) if k == 0 else [])
                        sch.add("tensor", lambda e, k=k, rhs=rhs, hb=hb:
                                e.matmul(fps[:, 512:1024],
                                         wu_sb[hb][:, k * 128:(k + 1) * 128],
                                         rhs, start=(k == 0), stop=(k == 7)),
                                inc=(TM, 1) if k == 7 else None)
                    mm_tm = TM.n
                    st["gumm"][(fp, ht)] = mm_tm
                    hw = [(TM, mm_tm),
                          (VV, st["hf_vv"].get((fp, ht - 2), 0) or
                           st["hf_vv"].get((fp - 1, ht + 22), 0))]

                    def gu_evac(e, hb=hb):
                        e.activation(gs_sb[hb][:, :], fps[:, 0:512], AF.Silu)
                        return e.copy(up_sb[hb][:, :], fps[:, 512:1024])
                    sch.add("scalar", gu_evac, waits=hw, inc=(SV, 1))
                    st["guev"][(fp, ht)] = SV.n
                    hfw = [(SV, st["guev"][(fp, ht)])]
                    if ht == 0:
                        hfw.append((TM, st["dmm"].get((fp - 1, 7), 0)))
                    sch.add("vector", lambda e, hb=hb, ht=ht: e.tensor_mul(
                        hf_sb[:, ht * FPW:(ht + 1) * FPW], gs_sb[hb][:, :],
                        up_sb[hb][:, :]), waits=hfw, inc=(VV, 1))
                    st["hf_vv"][(fp, ht)] = VV.n
                    if ht == 23:
                        st["hf_fp_vv"][fp] = VV.n

                def sched_dn_unit(fp, mt):
                    mb = 0
                    po = fp * FPW
                    wsrc = wdbf.ap()[ds(mt * 3072, 3072), :].rearrange(
                        "(a p) c -> p a c", p=128)
                    ww = [(TM, st["dmm"].get((fp, mt - 1), 0) or
                           st["dmm"].get((fp - 1, 7), 0))]
                    sch.add("sync", lambda e, s=wsrc, d=wdn_sb[mb]: dma(
                        e, d[:, :].rearrange("p (a c) -> p a c", c=128), s),
                        waits=ww, inc=(DX, 16))
                    wld = DX.n
                    if mt > 0:
                        pw = [(SV, st["dev"][(fp, mt - 1)])]
                    else:
                        pw = [(SV, st["guev"][(fp, 23)])]
                    for k in range(24):
                        sch.add("tensor", lambda e, k=k, mb=mb: e.matmul(
                            fps[:, 0:512],
                            wdn_sb[mb][:, k * 128:(k + 1) * 128],
                            hf_sb[:, k * FPW:(k + 1) * FPW],
                            start=(k == 0), stop=(k == 23)),
                            waits=([(DX, wld), (VV, st["hf_fp_vv"][fp])] + pw)
                            if k == 0 else [],
                            inc=(TM, 1) if k == 23 else None)
                    st["dmm"][(fp, mt)] = TM.n
                    sch.add("scalar", lambda e, mt=mt, po=po: e.activation(
                        g_sb[:, mt * TB + po:mt * TB + po + FPW],
                        fps[:, 0:512], AF.Silu),
                        waits=[(TM, st["dmm"][(fp, mt)])], inc=(SV, 1))
                    st["dev"][(fp, mt)] = SV.n

                ffn_units = []
                for fp in range(NFP):
                    for ht in range(24):
                        ffn_units.append(("gu", fp, ht))
                    for mt in range(8):
                        ffn_units.append(("dn", fp, mt))
                ffn_pos = [0]

                def emit_ffn(nmax):
                    n = 0
                    while n < nmax and ffn_pos[0] < len(ffn_units):
                        kind, a, b = ffn_units[ffn_pos[0]]
                        if kind == "gu":
                            sched_gu_unit(a, b)
                        else:
                            sched_dn_unit(a, b)
                        ffn_pos[0] += 1
                        n += 1

                sch.add("vector", lambda e: e.memset(hcar[:, :], 0.0),
                        waits=phase_w + [(CC, xp_ag_cc)], inc=(VV, 1))

                yev_vv = {}
                ydma_dg = {}
                scan_vv_d = {}
                gacc_d = {}
                yslv_d = {}
                dtb_s_sv = {}
                dtb_x_vv = {}
                dtrb_sv_d = {}
                dtr_tm_d = {}
                cev_sv = {}
                bev_sv = {}
                bmm_d = {}
                cmm_d = {}
                alast_sv = {}
                agy_cc = {}
                yslice_dg = {}

                for ch in range(NSC):
                    rk, half = ch // 2, ch % 2
                    colr = half * SCT
                    part = ch // 8
                    pcol = ch * SCT - part * (L // 2)
                    y_dst = y_inA if part == 0 else y_inB
                    for dtl in range(2):
                        def u_load(e, rk=rk, dtl=dtl, colr=colr):
                            row0 = rk * DSL + dtl * 128
                            return dma(e, u_sb[dtl][:, :],
                                       u_pri.ap()[row0:row0 + 128,
                                                  colr:colr + SCT])
                        uw = [(VV, yev_vv.get(ch - 1, 0))]
                        if ch == 0:
                            uw = list(phase_w) + [(DY, u_pri_dy)]
                        sch.add("sync", u_load, waits=uw, inc=(DX, 16))
                    u_ld = DX.n
                    bcw = [(CC, xp_ag_cc)] if ch == 0 else []
                    sch.add("sync", lambda e, rk=rk, colr=colr: dma(
                        e, b16[:, :],
                        xp_g.ap()[rk * 80:rk * 80 + 16, colr:colr + SCT]),
                        waits=bcw, inc=(DX, 16))
                    sch.add("sync", lambda e, rk=rk, colr=colr: dma(
                        e, c16[:, :],
                        xp_g.ap()[rk * 80 + 32:rk * 80 + 48, colr:colr + SCT]),
                        inc=(DX, 16))
                    sch.add("sync", lambda e, rk=rk, colr=colr: dma(
                        e, dtr_sb[:, :],
                        xp_g.ap()[rk * 80 + 64:rk * 80 + 65, colr:colr + SCT]),
                        waits=[(TM, dtr_tm_d.get(ch - 1, 0))], inc=(DX, 16))
                    xp_ld = DX.n
                    sch.add("vector", lambda e: e.tensor_copy(
                        b16b[:, :], b16[:, :]),
                        waits=[(DX, xp_ld), (TM, cmm_d.get((ch - 1, 15), 0))])
                    sch.add("vector", lambda e: e.tensor_copy(
                        c16b[:, :], c16[:, :]), inc=(VV, 1))
                    bc16_vv = VV.n

                    dtw = [(DX, xp_ld), (SV, bev_sv.get((ch - 1, 15), 0)),
                           (VV, dtb_x_vv.get((ch - 1, 1), 0))]
                    sch.add("tensor", lambda e: e.matmul(
                        quad0[:, 0:512], oner_sb[:, :], dtr_sb[:, 0:512],
                        start=True, stop=True), waits=dtw)
                    sch.add("tensor", lambda e: e.matmul(
                        quad0[:, 512:1024], oner_sb[:, :], dtr_sb[:, 512:1024],
                        start=True, stop=True), inc=(TM, 1))
                    dtr_tm = TM.n
                    dtr_tm_d[ch] = TM.n
                    for dtl in range(2):
                        def dt_x(e, dtl=dtl):
                            return e.tensor_scalar(
                                dt_sb[dtl][:, :], quad0[:, 0:1024],
                                wdt_sb[:, dtl:dtl + 1],
                                bdt_sb[:, dtl:dtl + 1], ALU.mult, ALU.add)
                        sch.add("vector", dt_x,
                                waits=[(TM, dtr_tm),
                                       (SV, alast_sv.get(ch - 1, 0))],
                                inc=(VV, 1))
                        dtb_x_vv[(ch, dtl)] = VV.n

                        def dt_s(e, dtl=dtl):
                            e.activation(sp2_sb[dtl][:, :], dt_sb[dtl][:, :],
                                         AF.Abs)
                            e.activation(sp2_sb[dtl][:, :], sp2_sb[dtl][:, :],
                                         AF.Exp, scale=-1.0)
                            return e.activation(sp2_sb[dtl][:, :],
                                                sp2_sb[dtl][:, :], AF.Ln,
                                                bias=1.0)
                        sch.add("scalar", dt_s,
                                waits=[(VV, dtb_x_vv[(ch, dtl)])],
                                inc=(SV, 1))
                        dtb_s_sv[(ch, dtl)] = SV.n

                        def dt_v(e, dtl=dtl):
                            e.tensor_relu(sp_sb[dtl][:, :], dt_sb[dtl][:, :])
                            e.tensor_add(dt_sb[dtl][:, :], sp_sb[dtl][:, :],
                                         sp2_sb[dtl][:, :])
                            return e.tensor_mul(w_sb[dtl][:, :],
                                                dt_sb[dtl][:, :],
                                                u_sb[dtl][:, :])
                        sch.add("vector", dt_v,
                                waits=[(SV, dtb_s_sv[(ch, dtl)]), (DX, u_ld)],
                                inc=(VV, 1))
                    w_vv = VV.n

                    for n in range(DS):
                        nb = n % 2
                        bw = [(SV, bev_sv.get((ch, n - 1), 0) or
                               bev_sv.get((ch - 1, 15), 0))]
                        if n == 0:
                            bw += [(VV, dtb_x_vv[(ch, 1)]), (VV, bc16_vv)]
                        sch.add("tensor", lambda e, n=n: e.matmul(
                            quad0[:, 0:512], oneh_bf[:, n * 128:(n + 1) * 128],
                            b16b[:, 0:512], start=True, stop=True),
                            waits=bw)
                        sch.add("tensor", lambda e, n=n: e.matmul(
                            quad0[:, 512:1024],
                            oneh_bf[:, n * 128:(n + 1) * 128],
                            b16b[:, 512:1024], start=True, stop=True),
                            inc=(TM, 1))
                        bmm = TM.n
                        bmm_d[(ch, n)] = TM.n
                        sch.add("scalar", lambda e, nb=nb: e.copy(
                            bb_sb[nb][:, :], quad0[:, 0:1024]),
                            waits=[(TM, bmm),
                                   (VV, scan_vv_d.get((ch, n - 2, 1), 0) or
                                    scan_vv_d.get((ch - 1, 15, 1), 0))],
                            inc=(SV, 1))
                        bev_sv[(ch, n)] = SV.n
                        cw_ = [(SV, cev_sv.get((ch, n - 1), 0) or
                                cev_sv.get((ch - 1, 15), 0))]
                        if n == 0:
                            cw_.append((VV, bc16_vv))
                        sch.add("tensor", lambda e, n=n: e.matmul(
                            quad0[:, 1024:1536],
                            oneh_bf[:, n * 128:(n + 1) * 128],
                            c16b[:, 0:512], start=True, stop=True),
                            waits=cw_)
                        sch.add("tensor", lambda e, n=n: e.matmul(
                            quad0[:, 1536:2048],
                            oneh_bf[:, n * 128:(n + 1) * 128],
                            c16b[:, 512:1024], start=True, stop=True),
                            inc=(TM, 1))
                        cmm = TM.n
                        cmm_d[(ch, n)] = TM.n
                        sch.add("scalar", lambda e, nb=nb: e.copy(
                            cc_sb[nb][:, :], quad0[:, 1024:2048]),
                            waits=[(TM, cmm),
                                   (VV, scan_vv_d.get((ch, n - 2, 1), 0) or
                                    scan_vv_d.get((ch - 1, 15, 1), 0))],
                            inc=(SV, 1))
                        cev_sv[(ch, n)] = SV.n
                        for dtl in range(2):
                            idx = dtl * 16 + n
                            ab = (2 * (n % 2)) + dtl
                            sch.add("scalar", lambda e, dtl=dtl, idx=idx,
                                    ab=ab: e.activation(
                                        a_sb[ab][:, :], dt_sb[dtl][:, :],
                                        AF.Exp,
                                        scale=an_sb[:, idx:idx + 1]),
                                    waits=[(VV, w_vv),
                                           (VV, scan_vv_d.get(
                                               (ch, n - 2, dtl), 0) or
                                            scan_vv_d.get(
                                                (ch - 1, 15, dtl), 0))],
                                    inc=(SV, 1))
                            a_sv = SV.n

                            def scan_ops(e, dtl=dtl, idx=idx, nb=nb, ab=ab,
                                         ch=ch):
                                e.tensor_mul(be_sb[:, :], w_sb[dtl][:, :],
                                             bb_sb[nb][:, :])
                                e.tensor_tensor_scan(
                                    h_sb[:, :], a_sb[ab][:, :], be_sb[:, :],
                                    0.0 if ch == 0 else hcar[:, idx:idx + 1],
                                    ALU.mult, ALU.add)
                                e.tensor_mul(hc_sb[dtl][:, :], h_sb[:, :],
                                             cc_sb[nb][:, :])
                                return e.tensor_copy(hcar[:, idx:idx + 1],
                                                     h_sb[:, SCT - 1:SCT])
                            prev_g = gacc_d.get((ch, n - 1, dtl), 0) or \
                                gacc_d.get((ch - 1, 15, dtl), 0)
                            sch.add("vector", scan_ops,
                                    waits=[(SV, a_sv), (SV, bev_sv[(ch, n)]),
                                           (SV, cev_sv[(ch, n)]),
                                           (GP, prev_g),
                                           (VV, scan_vv_d.get(
                                               (ch - 1, n, dtl), 0))],
                                    inc=(VV, 1))
                            scan_vv_d[(ch, n, dtl)] = VV.n
                            ya = y_acc[:, dtl * 1024:(dtl + 1) * 1024]
                            gw = [(VV, scan_vv_d[(ch, n, dtl)])]
                            if n == 0:
                                gw.append((VV, yslv_d.get((ch - 1, dtl), 0)))

                            def gacc(e, ya=ya, dtl=dtl, n=n):
                                if n == 0:
                                    return e.tensor_copy(ya, hc_sb[dtl][:, :])
                                return e.tensor_add(ya, ya, hc_sb[dtl][:, :])
                            sch.add("gpsimd", gacc, waits=gw, inc=(GP, 1))
                            gacc_d[(ch, n, dtl)] = GP.n
                        if n in (3, 7, 11, 15):
                            emit_ffn(2)
                    alast_sv[ch] = SV.n

                    for dtl in range(2):
                        sch.add("vector", lambda e, dtl=dtl: e.tensor_copy(
                            ysl[dtl][:, :],
                            y_acc[:, dtl * 1024:(dtl + 1) * 1024]),
                            waits=[(GP, gacc_d[(ch, 15, dtl)]),
                                   (DG, ydma_dg.get(ch - 1, 0))],
                            inc=(VV, 1))
                        yv = VV.n
                        yslv_d[(ch, dtl)] = VV.n
                        sch.add("gpsimd", lambda e, dtl=dtl, y_dst=y_dst,
                                pcol=pcol: dma(
                                    e, y_dst.ap()[dtl * 128:dtl * 128 + 128,
                                                  pcol:pcol + SCT],
                                    ysl[dtl][:, :]),
                                waits=[(VV, yv)], inc=(DG, 16))
                    yev_vv[ch] = VV.n
                    ydma_dg[ch] = DG.n

                emit_ffn(len(ffn_units))
                assert ffn_pos[0] == len(ffn_units)

                mid_done_w = [(DG, DG.n), (SV, SV.n), (VV, VV.n),
                              (TM, TM.n), (DX, DX.n)]
                for p, (yi, yg) in enumerate(((y_inA, y_gA),
                                              (y_inB, y_gB))):
                    sch.add("gpsimd", lambda e, yi=yi, yg=yg:
                            e.collective_compute(
                                "AllGather", ALU.bypass,
                                replica_groups=[list(range(NC))],
                                ins=[yi.ap().opt()],
                                outs=[yg.ap().opt()]),
                            waits=mid_done_w if p == 0 else [], inc=(CC, 1))
                    agy_cc[p] = CC.n

                def y_slice_a(e):
                    reg = nc.alloc_register(e.engine, "yca_g")
                    e.reg_load(reg, yo_sb[0:1, 0:1])
                    yca = e.snap(reg, min_val=0, max_val=L // 2 - TB)
                    return dma(e, y_pri2.ap()[:, 0:TB],
                               y_gA.ap()[:, ds(yca, TB)])
                sch.add("gpsimd", y_slice_a,
                        waits=[(CC, agy_cc[0]), (DY, DY.n)], inc=(DY, 16))

                def y_slice_b(e):
                    reg = nc.alloc_register(e.engine, "ycb_g")
                    e.reg_load(reg, yo_sb[0:1, 1:2])
                    ycb = e.snap(reg, min_val=0, max_val=L // 2 - TB)
                    return dma(e, y_pri2.ap()[:, TB:2 * TB],
                               y_gB.ap()[:, ds(ycb, TB)])
                sch.add("gpsimd", y_slice_b,
                        waits=[(CC, agy_cc[1]), (DY, DY.n)], inc=(DY, 16))

                def y_sel(e):
                    reg = nc.alloc_register(e.engine, "ysel_g")
                    e.reg_load(reg, yo_sb[0:1, 2:3])
                    ysel = e.snap(reg, min_val=0, max_val=TB)
                    return dma(e, y_pri.ap()[:, :],
                               y_pri2.ap()[:, ds(ysel, TB)])
                sch.add("gpsimd", y_sel, waits=[(DY, DY.n)], inc=(DY, 16))
                yslice_dg[1] = DY.n
                if DEBUG:
                    sch.add("gpsimd", lambda e: dma(
                        e, dbg_y2.ap()[:, :], y_pri2.ap()[:, :]),
                        waits=[(DY, yslice_dg[1])], inc=(DG, 16))
                    sch.add("gpsimd", lambda e: dma(
                        e, dbg_yin.ap()[:, 0:L // 2], y_inA.ap()[:, :]),
                        waits=[(DG, ydma_dg[7])], inc=(DG, 16))
                    sch.add("gpsimd", lambda e: dma(
                        e, dbg_yin.ap()[:, L // 2:L], y_inB.ap()[:, :]),
                        waits=[(DG, ydma_dg[15])], inc=(DG, 16))
                    sch.add("gpsimd", lambda e: dma(
                        e, dbg_g.ap()[:, :].rearrange(
                            "p (a t) -> p a t", t=TB),
                        g_sb[:, :].rearrange("p (a t) -> p a t", t=TB)),
                        waits=[(SV, SV.n)], inc=(DG, 16))
                mid_vv, mid_sv, mid_tm, mid_dx = VV.n, SV.n, TM.n, DX.n
            xnctx.close()

            with ExitStack() as ectx:
                SBe = lambda n, s, d: ectx.enter_context(nc.sbuf_tensor(n, s, d))
                yt_sb = SBe("yt_sb", [128, 16 * 512], BF16)
                xc2_sb = SBe("xc2_sb", [128, 16 * 512], BF16)
                zs2_sb = SBe("zs2_sb", [128, 16 * 512], BF16)
                y3_sb = [SBe(f"y3_sb{i}", [128, 16 * 512], BF16)
                         for i in range(2)]
                xt2_sb = [SBe(f"xt2_sb{i}", [128, 8 * 512], F32)
                          for i in range(2)]
                wob_sb = [SBe(f"wob_sb{i}", [128, 16 * 128], BF16)
                          for i in range(2)]
                p_sb = SBe("p_sb", [128, 512], BF16)
                oth_sb = SBe("oth_sb", [128, 8 * 512], F32)

                pe_phase_w = [(VV, mid_vv), (SV, mid_sv), (TM, mid_tm),
                              (DX, mid_dx), (DY, yslice_dg[1])]
                wo_mm_tm = {}
                y3_vv_d = {}
                pout_vv = {}
                pout_last = {}
                out_dg = {}
                yt_dg = {}
                ld_dx = {}

                pending_out = []

                def emit_out(sub):
                    c0 = sub * 512
                    sch.add("gpsimd", lambda e, c0=c0: dma(
                        e, ot.ap()[:, c0:c0 + 512].rearrange(
                            "(a p) t -> p a t", p=128),
                        oth_sb[:, :].rearrange("p (a t) -> p a t", t=512)),
                        waits=[(VV, pout_last[sub])], inc=(DG, 16))
                    out_dg[sub] = DG.n

                for sub in range(4):
                    c0 = sub * 512
                    eb = sub % 2

                    ylw = [(VV, y3_vv_d.get(sub - 1, 0))]
                    if sub == 0:
                        ylw = list(pe_phase_w)
                    sch.add("sync", lambda e, c0=c0: dma(
                        e, yt_sb[:, :].rearrange("p (a t) -> p a t", t=512),
                        y_pri.ap()[:, c0:c0 + 512].rearrange(
                            "(a p) t -> p a t", p=128)),
                        waits=ylw, inc=(DX, 16))
                    yt_dg[sub] = DX.n
                    if sub >= 1:
                        emit_out(sub - 1)

                    slw = []
                    if sub == 0:
                        parts = [(0, 0, 448), (1, 0, 64)]
                    else:
                        parts = [(sub, 64, 448), (sub + 1, 0, 64)]
                    dof = 0
                    for (pk, so, sw) in parts:
                        sch.add("sync", lambda e, pk=pk, so=so, sw=sw,
                                dof=dof: dma(
                                    e, xc2_sb[:, :].rearrange(
                                        "p (a t) -> p a t",
                                        t=512)[:, :, dof:dof + sw],
                                    xcv_p[pk].ap()[:, so:so + sw].rearrange(
                                        "(a p) t -> p a t", p=128)),
                                waits=slw, inc=(DX, 16))
                        slw = []
                        dof += sw
                    sch.add("sync", lambda e, c0=c0: dma(
                        e, zs2_sb[:, :].rearrange("p (a t) -> p a t", t=512),
                        zs_pri.ap()[:, c0:c0 + 512].rearrange(
                            "(a p) t -> p a t", p=128)), inc=(DX, 16))
                    sch.add("sync", lambda e, c0=c0, eb=eb: dma(
                        e, xt2_sb[eb][:, :].rearrange(
                            "p (a t) -> p a t", t=512),
                        xt.ap()[:, EX + c0:EX + c0 + 512].rearrange(
                            "(a p) t -> p a t", p=128)), inc=(DX, 16))
                    ld_dx[sub] = DX.n

                    for dt16 in range(16):
                        def comb(e, dt16=dt16, eb=eb):
                            e.scalar_tensor_tensor(
                                y3_sb[eb][:, dt16 * 512:(dt16 + 1) * 512],
                                xc2_sb[:, dt16 * 512:(dt16 + 1) * 512],
                                dp_sb[:, dt16:dt16 + 1],
                                yt_sb[:, dt16 * 512:(dt16 + 1) * 512],
                                ALU.mult, ALU.add)
                            return e.tensor_mul(
                                y3_sb[eb][:, dt16 * 512:(dt16 + 1) * 512],
                                y3_sb[eb][:, dt16 * 512:(dt16 + 1) * 512],
                                zs2_sb[:, dt16 * 512:(dt16 + 1) * 512])
                        wts = []
                        if dt16 == 0:
                            wts = [(DX, ld_dx[sub]),
                                   (TM, wo_mm_tm.get((sub - 2, 7), 0))]
                        sch.add("vector", comb, waits=wts,
                                inc=(VV, 1) if dt16 == 15 else None)
                    y3_vv_d[sub] = VV.n

                    for mt in range(8):
                        mb = mt % 2
                        wsrc = wobf.ap()[ds(mt * 2048, 2048), :].rearrange(
                            "(a p) c -> p a c", p=128)
                        ww = [(TM, wo_mm_tm.get((sub, mt - 2), 0) or
                               wo_mm_tm.get((sub - 1, mt + 6), 0))]
                        sch.add("sync", lambda e, s=wsrc, mb=mb: dma(
                            e, wob_sb[mb][:, :].rearrange(
                                "p (a c) -> p a c", c=128), s),
                            waits=ww, inc=(DX, 16))
                        wld = DX.n
                        slot = mt % 4
                        ps = quad0[:, slot * 512:(slot + 1) * 512]
                        pw = [(VV, pout_vv.get((sub, mt - 4), 0) or
                               pout_vv.get((sub - 1, mt + 4), 0))]
                        for k in range(16):
                            wts = []
                            if k == 0:
                                wts = [(VV, y3_vv_d[sub]), (DX, wld)] + pw
                            sch.add("tensor", lambda e, ps=ps, mb=mb, k=k,
                                    eb=eb: e.matmul(
                                        ps,
                                        wob_sb[mb][:, k * 128:(k + 1) * 128],
                                        y3_sb[eb][:, k * 512:(k + 1) * 512],
                                        start=(k == 0), stop=(k == 15)),
                                    waits=wts,
                                    inc=(TM, 1) if k == 15 else None)
                        wo_mm_tm[(sub, mt)] = TM.n

                        def pout(e, ps=ps, mt=mt, sub=sub, eb=eb, c0=c0):
                            e.tensor_mul(p_sb[:, :], ps,
                                         g_sb[:, mt * TB + c0:
                                              mt * TB + c0 + 512])
                            return e.tensor_add(
                                oth_sb[:, mt * 512:(mt + 1) * 512],
                                xt2_sb[eb][:, mt * 512:(mt + 1) * 512],
                                p_sb[:, :])
                        wts = [(TM, wo_mm_tm[(sub, mt)])]
                        if mt == 0:
                            wts.append((DG, out_dg.get(sub - 1, 0)))
                        sch.add("vector", pout, waits=wts, inc=(VV, 1))
                        pout_vv[(sub, mt)] = VV.n
                    pout_last[sub] = VV.n

                emit_out(3)
                sch.add("gpsimd", lambda e: e.engine_nop(),
                        waits=[(DG, DG.n)])

        with nc.Block() as block:
            @block.sync
            def _(eng):
                sch.emit(eng, "sync")

            @block.gpsimd
            def _(eng):
                sch.emit(eng, "gpsimd")

            @block.vector
            def _(eng):
                sch.emit(eng, "vector")

            @block.scalar
            def _(eng):
                sch.emit(eng, "scalar")

            @block.tensor
            def _(eng):
                sch.emit(eng, "tensor")

    return nc


_NC_CACHE = [None]
_LAST_RES = [None]


def kernel(L0_attn_x, L0_attn_W_in, L0_attn_conv_w, L0_attn_conv_b,
           L0_attn_W_x, L0_attn_W_dt, L0_attn_b_dt, L0_attn_A_log,
           L0_attn_D_param, L0_attn_W_out, L0_ffn_w_gate, L0_ffn_w_up,
           L0_ffn_w_down):
    x = np.asarray(L0_attn_x)
    xT = np.ascontiguousarray(x.reshape(L, DM).T)
    wiT = np.ascontiguousarray(np.asarray(L0_attn_W_in).T)
    wit_t = np.ascontiguousarray(
        wiT.reshape(8, 128, 32, 128).transpose(2, 0, 1, 3)).reshape(-1, 128)
    wx = np.asarray(L0_attn_W_x)
    wxt = np.zeros((DI, 80), np.float32)
    wxt[:, 0:16] = wx[0:16, :].T
    wxt[:, 32:48] = wx[16:32, :].T
    wxt[:, 64:65] = wx[32:33, :].T
    cw = np.ascontiguousarray(np.asarray(L0_attn_conv_w).reshape(DI, 4))
    cb = np.ascontiguousarray(np.asarray(L0_attn_conv_b).reshape(DI, 1))
    dparam = np.ascontiguousarray(np.asarray(L0_attn_D_param).reshape(DI, 1))
    wdt_full = np.asarray(L0_attn_W_dt).reshape(DI, 1)
    bdt_full = np.asarray(L0_attn_b_dt).reshape(DI, 1)
    alog_full = np.asarray(L0_attn_A_log)
    wg = np.asarray(L0_ffn_w_gate)
    wu = np.asarray(L0_ffn_w_up)
    wd = np.asarray(L0_ffn_w_down)
    wg_t = np.ascontiguousarray(
        wg.T.reshape(8, 128, 24, 128).transpose(2, 0, 1, 3)).reshape(-1, 128)
    wu_t = np.ascontiguousarray(
        wu.T.reshape(8, 128, 24, 128).transpose(2, 0, 1, 3)).reshape(-1, 128)
    wd_t = np.ascontiguousarray(
        wd.T.reshape(24, 128, 8, 128).transpose(2, 0, 1, 3)).reshape(-1, 128)
    woT = np.ascontiguousarray(np.asarray(L0_attn_W_out).T)
    wot_t = np.ascontiguousarray(
        woT.reshape(16, 128, 8, 128).transpose(2, 0, 1, 3)).reshape(-1, 128)
    ones_cv = np.ones((128, 1), np.float32)
    ones_rv = np.ones((1, 128), np.float32)
    identv = np.eye(128, dtype=np.float32)
    onehv = np.zeros((16, 16 * 128), np.float32)
    for n in range(16):
        onehv[n, n * 128:(n + 1) * 128] = 1.0

    if _NC_CACHE[0] is None:
        _NC_CACHE[0] = build_nc()
    nc = _NC_CACHE[0]

    in_maps = []
    for c in range(NC):
        xt_c = np.zeros((DM, TA), np.float32)
        t0 = c * TB
        xt_c[:, EX:] = xT[:, t0:t0 + TB]
        lo = max(0, t0 - EX)
        n_pre = t0 - lo
        if n_pre > 0:
            xt_c[:, EX - n_pre:EX] = xT[:, lo:t0]
        dsl = slice(c * DSL, (c + 1) * DSL)
        yca = c * TB if c < 4 else 0
        ycb = (c - 4) * TB if c >= 4 else 0
        ysel = 0 if c < 4 else TB
        in_maps.append({
            "xt": xt_c,
            "wit": wit_t,
            "wxt": wxt,
            "cwt": cw, "cbt": cb, "dpar": dparam,
            "wdt": np.ascontiguousarray(wdt_full[dsl]),
            "bdt": np.ascontiguousarray(bdt_full[dsl]),
            "alog": np.ascontiguousarray(alog_full[dsl]),
            "wg_t": wg_t, "wu_t": wu_t, "wd_t": wd_t,
            "wot": wot_t,
            "ones_c": ones_cv, "ones_r": ones_rv, "ident": identv,
            "oneh": onehv,
            "cvec": np.array([[c]], np.int32),
            "yoff": np.array([[yca, ycb, ysel]], np.int32),
        })

    res = run_bass_kernel_spmd(nc, in_maps, core_ids=list(range(NC)))
    _LAST_RES[0] = res
    out = np.empty((L, DM), np.float32)
    for c in range(NC):
        out[c * TB:(c + 1) * TB, :] = np.asarray(res.results[c]["ot"]).T
    return out.reshape(1, L, DM)
